# version 08
# baseline (speedup 1.0000x reference)
import numpy as np
import ml_dtypes
import concourse.bass as bass
import concourse.mybir as mybir
import concourse.bass_utils as _bass_utils
from concourse.bass_utils import run_bass_kernel_spmd

if not getattr(_bass_utils, "_ldw_opt_patched", False):
    _orig_run_command = _bass_utils.run_command

    def _run_command_ldw(cmd, *a, **kw):
        if isinstance(cmd, list):
            cmd = ["--enable-ldw-opt=true" if c == "--enable-ldw-opt=false" else c
                   for c in cmd]
        return _orig_run_command(cmd, *a, **kw)

    _bass_utils.run_command = _run_command_ldw
    _bass_utils._ldw_opt_patched = True

F32 = mybir.dt.float32
BF16 = mybir.dt.bfloat16
I32 = mybir.dt.int32
AF = mybir.ActivationFunctionType
ALU = mybir.AluOpType
AX = mybir.AxisListType

PI = float(np.pi)

SEQ = 2
T = 4096
D = 1024
DH = 2048
NB = 8
E = 8
FF = 1408
L1 = 128
NC1 = SEQ * T // L1
L2 = 512
NT2 = L2 // 128
NC2 = SEQ * T // L2
PH2_LAG = 5

C_ONE, C_NEG1, C_HALF, C_NEGHALF, C_PIH, C_EPS, C_NEG2PI, C_I2PI = range(8)

ENGS = ("sync", "gpsimd", "tensor", "vector", "scalar")


class Ctr:
    def __init__(self, nc, name):
        self.sem = nc.alloc_semaphore(name)
        self.n = 0


class Sched:
    def __init__(self):
        self.q = {e: [] for e in ENGS}

    def on(self, eng, fn):
        self.q[eng].append(fn)

    def wait(self, eng, ctr, val=None):
        v = ctr.n if val is None else val
        if v <= 0:
            return
        sem = ctr.sem
        self.on(eng, lambda e, s=sem, vv=v: e.wait_ge(s, vv))

    def op(self, eng, fn, inc=None, k=1):
        if inc is not None:
            inc.n += k
            sem = inc.sem
            self.on(eng, lambda e, f=fn, s=sem, kk=k: f(e).then_inc(s, kk))
        else:
            self.on(eng, fn)

    def dma(self, eng, out, in_, inc):
        inc.n += 16
        sem = inc.sem
        self.on(eng, lambda e, o=out, i=in_, s=sem: e.dma_start(out=o, in_=i).then_inc(s, 16))

    def dmat(self, eng, out, in_, inc):
        inc.n += 16
        sem = inc.sem
        self.on(eng, lambda e, o=out, i=in_, s=sem: e.dma_start_transpose(o, i).then_inc(s, 16))

    def emit(self, nc, name):
        with nc.Block(name) as block:
            @block.sync
            def _(e):
                for fn in self.q["sync"]:
                    fn(e)

            @block.gpsimd
            def _(e):
                for fn in self.q["gpsimd"]:
                    fn(e)

            @block.tensor
            def _(e):
                for fn in self.q["tensor"]:
                    fn(e)

            @block.vector
            def _(e):
                for fn in self.q["vector"]:
                    fn(e)

            @block.scalar
            def _(e):
                for fn in self.q["scalar"]:
                    fn(e)


def build_nc(debug=False, dbgc2=0):
    nc = bass.Bass("TRN2", target_bir_lowering=False)

    x_h = nc.dram_tensor("x", [SEQ, T, D], F32, kind="ExternalInput").ap()
    w_in_h = nc.dram_tensor("w_in", [D, DH], F32, kind="ExternalInput").ap()
    b_in_h = nc.dram_tensor("b_in", [128, 16], F32, kind="ExternalInput").ap()
    w_gx_h = nc.dram_tensor("w_gx", [NB, 256, 128], F32, kind="ExternalInput").ap()
    b_gx_h = nc.dram_tensor("b_gx", [128, 8], F32, kind="ExternalInput").ap()
    w_ga_h = nc.dram_tensor("w_ga", [NB, 256, 128], F32, kind="ExternalInput").ap()
    b_ga_h = nc.dram_tensor("b_ga", [128, 8], F32, kind="ExternalInput").ap()
    arp_h = nc.dram_tensor("arp", [128, 8], F32, kind="ExternalInput").ap()
    aip_h = nc.dram_tensor("aip", [128, 8], F32, kind="ExternalInput").ap()
    w_out_h = nc.dram_tensor("w_out", [DH, D], F32, kind="ExternalInput").ap()
    bout_pl_h = nc.dram_tensor("bout_pl", [128, D], F32, kind="ExternalInput").ap()
    rw_h = nc.dram_tensor("rw", [D, E], F32, kind="ExternalInput").ap()
    w1_h = nc.dram_tensor("w1", [E, D, 2 * FF], F32, kind="ExternalInput").ap()
    w2_h = nc.dram_tensor("w2", [E, FF, D], F32, kind="ExternalInput").ap()
    cc_h = nc.dram_tensor("cc", [128, 8], F32, kind="ExternalInput").ap()

    y_h = nc.dram_tensor("y", [SEQ, T, D], F32, kind="ExternalOutput").ap()

    _scr = "ExternalOutput" if debug else "Internal"
    w1bf_h = nc.dram_tensor("w1bf", [E, 11, 128, 2, 8, 128], BF16, kind=_scr).ap()
    w2bf_h = nc.dram_tensor("w2bf", [E, 2, 128, 11, 512], BF16, kind=_scr).ap()
    pscr_h = nc.dram_tensor("pscr", [NC1, 128, D], F32, kind=_scr).ap()
    if debug:
        dbg_lp = nc.dram_tensor("dbg_lp", [128, NT2, E], F32, kind="ExternalOutput").ap()
        dbg_pr4 = nc.dram_tensor("dbg_pr4", [128, NT2, E], F32, kind="ExternalOutput").ap()
        dbg_sm = nc.dram_tensor("dbg_sm", [128, NT2], F32, kind="ExternalOutput").ap()
        dbg_xn2 = nc.dram_tensor("dbg_xn2", [128, 8, L2], BF16, kind="ExternalOutput").ap()
        dbg_part2 = nc.dram_tensor("dbg_part2", [128, NT2, D], F32, kind="ExternalOutput").ap()
        dbg_g = nc.dram_tensor("dbg_g", [128, 11, L2], BF16, kind="ExternalOutput").ap()
        dbg_p2 = nc.dram_tensor("dbg_p2", [128, NT2, D], F32, kind="ExternalOutput").ap()

    cc = nc.alloc_sbuf_tensor("cc_sb", [128, 8], F32)
    bout_pl = nc.alloc_sbuf_tensor("bout_sb", [128, D], F32)
    rw_sb = nc.alloc_sbuf_tensor("rw_sb", [128, 8, E], BF16)
    bin_c = nc.alloc_sbuf_tensor("bin_c", [128, 16], F32)
    bgx_c = nc.alloc_sbuf_tensor("bgx_c", [128, 8], F32)
    bga_c = nc.alloc_sbuf_tensor("bga_c", [128, 8], F32)
    arp_c = nc.alloc_sbuf_tensor("arp_c", [128, 8], F32)
    aip_c = nc.alloc_sbuf_tensor("aip_c", [128, 8], F32)
    msp_c = nc.alloc_sbuf_tensor("msp_c", [128, 8], F32)
    tc0 = nc.alloc_sbuf_tensor("tc0", [128, 8], F32)
    tc1 = nc.alloc_sbuf_tensor("tc1", [128, 8], F32)
    tc2 = nc.alloc_sbuf_tensor("tc2", [128, 8], F32)
    tc3 = nc.alloc_sbuf_tensor("tc3", [128, 8], F32)
    bar = nc.alloc_sbuf_tensor("bar", [128, 8], F32)
    ones_pl = nc.alloc_sbuf_tensor("ones_pl", [128, L1], F32)
    lnc = nc.alloc_sbuf_tensor("lnc", [128, 8], F32)
    s2c = nc.alloc_sbuf_tensor("s2c", [128, 1], F32)
    carry_r = nc.alloc_sbuf_tensor("carry_r", [128, 8], F32)
    carry_i = nc.alloc_sbuf_tensor("carry_i", [128, 8], F32)

    w_in_sb = nc.alloc_sbuf_tensor("w_in_sb", [128, 8, DH], BF16)
    wgx_sb = nc.alloc_sbuf_tensor("wgx_sb", [128, NB, 2, 128], BF16)
    wga_sb = nc.alloc_sbuf_tensor("wga_sb", [128, NB, 2, 128], BF16)
    w_out_sb = nc.alloc_sbuf_tensor("w_out_sb", [128, 16, D], BF16)
    x_tm2 = [nc.alloc_sbuf_tensor(f"x_tm{i}", [128, D], F32) for i in range(2)]
    xn_tm = nc.alloc_sbuf_tensor("xn_tm", [128, D], BF16)
    xh_f = nc.alloc_sbuf_tensor("xh_f", [128, 16, L1], F32)
    GA = nc.alloc_sbuf_tensor("GA", [128, 8, L1], F32)
    GX = nc.alloc_sbuf_tensor("GX", [128, 8, L1], F32)
    T0 = nc.alloc_sbuf_tensor("T0", [128, 8, L1], F32)
    T1 = nc.alloc_sbuf_tensor("T1", [128, 8, L1], F32)
    T2 = nc.alloc_sbuf_tensor("T2", [128, 8, L1], F32)
    T3 = nc.alloc_sbuf_tensor("T3", [128, 8, L1], F32)
    T4 = nc.alloc_sbuf_tensor("T4", [128, 8, L1], F32)
    T5 = nc.alloc_sbuf_tensor("T5", [128, 8, L1], F32)
    partial = nc.alloc_sbuf_tensor("partial", [128, D], F32)

    KIap = T5.ap().bitcast(I32)
    h_bf = T1.ap().bitcast(BF16).rearrange("p c (x t) -> p (c x) t", x=2)
    xh_b = T3.ap().bitcast(BF16).rearrange("p c (x t) -> p (c x) t", x=2)
    sqv = T0.ap().rearrange("p a b -> p (a b)")

    W1R = 2
    w1r = nc.alloc_sbuf_tensor("w1r", [128, W1R, 2, 8, 128], BF16)
    W2R = 2
    w2r = nc.alloc_sbuf_tensor("w2r", [128, W2R, 11, 512], BF16)
    gbuf = nc.alloc_sbuf_tensor("gbuf", [128, 11, L2], BF16)
    silu_sb = [nc.alloc_sbuf_tensor("silu_0", [128, L2], F32)] * 2
    xn2 = [nc.alloc_sbuf_tensor(f"xn2_{i}", [128, 8, L2], BF16) for i in range(2)]
    part2 = nc.alloc_sbuf_tensor("part2", [128, NT2, D], F32)
    lp = nc.alloc_sbuf_tensor("lp", [128, NT2, E], F32)
    pr4 = nc.alloc_sbuf_tensor("pr4", [128, NT2, E], F32)
    sm0 = nc.alloc_sbuf_tensor("sm0", [128, NT2], F32)
    sm1 = nc.alloc_sbuf_tensor("sm1", [128, NT2], F32)

    p_a0 = nc.alloc_psum_tensor("p_a0", [128, 512], F32)
    p_a1 = nc.alloc_psum_tensor("p_a1", [128, 512], F32)
    p_y00 = nc.alloc_psum_tensor("p_y00", [128, 512], F32)
    p_y01 = nc.alloc_psum_tensor("p_y01", [128, 512], F32)
    p_y10 = nc.alloc_psum_tensor("p_y10", [128, 512], F32)
    p_y11 = nc.alloc_psum_tensor("p_y11", [128, 512], F32)
    p_o0 = nc.alloc_psum_tensor("p_o0", [128, 512], F32)
    p_o1 = nc.alloc_psum_tensor("p_o1", [128, 512], F32)
    p_a = [p_a0, p_a1]
    p_y = [[p_y00, p_y01], [p_y10, p_y11]]
    p_o = [p_o0, p_o1]

    s_ld = Ctr(nc, "s_ld")
    s_gw = Ctr(nc, "s_gw")
    s_pe = Ctr(nc, "s_pe")
    s_dv = Ctr(nc, "s_dv")
    s_ac = Ctr(nc, "s_ac")
    s_st = Ctr(nc, "s_st")
    s_w1s = [Ctr(nc, "s_w1s0"), Ctr(nc, "s_w1s1")]
    s_w2s = [Ctr(nc, "s_w2s0"), Ctr(nc, "s_w2s1")]
    s_p2l = Ctr(nc, "s_p2l")
    s_gp = Ctr(nc, "s_gp")

    def ccol(i):
        return cc[:, i:i + 1]

    s = Sched()

    def vflush(inc):
        s.op("vector", lambda v: v.tensor_copy(bar[:, 2:3], cc[:, 0:1]), inc=inc)

    def aflush(inc):
        s.op("scalar", lambda sc: sc.copy(bar[:, 3:4], cc[:, 0:1]), inc=inc)

    def vsp():
        s.op("vector", lambda v: v.tensor_copy(bar[:, 5:6], cc[:, 1:2]))

    def gflush(inc):
        s.op("gpsimd", lambda g: g.tensor_copy(bar[:, 6:7], cc[:, 0:1]), inc=inc)

    s.dma("sync", cc[:], cc_h, s_ld)
    s.dma("sync", bout_pl[:], bout_pl_h, s_ld)
    s.dma("sync", bin_c[:], b_in_h, s_ld)
    s.dma("sync", bgx_c[:], b_gx_h, s_ld)
    s.dma("sync", bga_c[:], b_ga_h, s_ld)
    s.dma("sync", arp_c[:], arp_h, s_ld)
    s.dma("sync", aip_c[:], aip_h, s_ld)
    LD_CONSTS = s_ld.n

    s.dma("gpsimd", w_in_sb[:], w_in_h.rearrange("(k p) h -> p k h", p=128), s_gw)
    s.dma("gpsimd", wgx_sb[:], w_gx_h.rearrange("b (kt p) o -> p b kt o", p=128), s_gw)
    s.dma("gpsimd", wga_sb[:], w_ga_h.rearrange("b (kt p) o -> p b kt o", p=128), s_gw)
    s.dma("gpsimd", w_out_sb[:], w_out_h.rearrange("(k p) h -> p k h", p=128), s_gw)
    s.dma("gpsimd", rw_sb[:], rw_h.rearrange("(k p) e -> p k e", p=128), s_gw)
    GW_ATTN = s_gw.n
    GW_CONV_E = []
    for e in range(E):
        w1r_in = w1_h[e].rearrange("(kt p) (half fp f) -> fp p half kt f",
                                   p=128, half=2, f=128)
        for fp in range(11):
            for half in range(2):
                s.dma("gpsimd", w1bf_h[e, fp, :, half], w1r_in[fp, :, half], s_gw)
        w2r_in = w2_h[e].rearrange("(ft p) (hh h) -> hh p ft h", p=128, hh=2)
        for hh in range(2):
            s.dma("gpsimd", w2bf_h[e, hh], w2r_in[hh], s_gw)
        GW_CONV_E.append(s_gw.n)

    s.wait("vector", s_ld, LD_CONSTS)
    s.op("vector", lambda v: v.memset(ones_pl[:], 1.0))
    s.op("vector", lambda v: v.memset(carry_r[:], 0.0))
    s.op("vector", lambda v: v.memset(carry_i[:], 0.0), inc=s_dv)
    V_INIT = s_dv.n
    s.wait("scalar", s_ld, LD_CONSTS)
    s.op("scalar", lambda sc: sc.activation(tc0[:], arp_c[:], AF.Abs))
    s.op("scalar", lambda sc: sc.activation(tc3[:], arp_c[:], AF.Relu))
    s.op("scalar", lambda sc: sc.activation(tc1[:], tc0[:], AF.Exp, scale=ccol(C_NEG1)))
    s.op("scalar", lambda sc: sc.copy(bar[:, 4:5], cc[:, 0:1]))
    s.op("scalar", lambda sc: sc.activation(tc0[:], tc1[:], AF.Ln, bias=ccol(C_ONE)))
    aflush(s_ac)
    A_SP = s_ac.n
    s.wait("vector", s_ac, A_SP)
    s.op("vector", lambda v: v.tensor_add(msp_c[:], tc0[:], tc3[:]))
    vflush(s_dv)
    s.op("vector", lambda v: v.tensor_single_scalar(msp_c[:], msp_c[:], -8.0, ALU.mult))
    vflush(s_dv)
    V_MSP = s_dv.n

    pa_free = [("", 0), ("", 0)]
    p1 = dict(V_end=0, L_t_prev=0, P_bdl_prev=0, P_wo_prev=0,
              S_p_prev=0, mm_rot=0, V_h=0, c_front=-1, G_carry=0)
    Vend_hist = [0] * (NC1 + 2)
    S_pscr = [0] * NC1
    S_xnt = [0] * NC1
    p2 = dict(py_free=[0, 0], po_free=[0, 0], xn2_freeby=[0, 0],
              w1_free=[0] * 2, w2_free=[0] * 2, S_part_prev=0,
              gft=0, ghh=0, gtt=0, V_lp=0)
    W1R_, W2R_ = 2, 2

    x_r = x_h.rearrange("s (c p) ch -> (s c) p ch", p=128)
    y_r = y_h.rearrange("s (k tt p) ch -> (s k) p tt ch", tt=NT2, p=128)

    xr = xh_f.ap().rearrange("p (a ct) t -> p a ct t", a=2)[:, 0]
    xi = xh_f.ap().rearrange("p (a ct) t -> p a ct t", a=2)[:, 1]
    hr = h_bf.rearrange("p (a ct) t -> p a ct t", a=2)[:, 0]
    hi = h_bf.rearrange("p (a ct) t -> p a ct t", a=2)[:, 1]
    mspb = msp_c.ap().broadcast_to([128, 8, L1])
    aipb = aip_c.ap().broadcast_to([128, 8, L1])
    prv = p_o0.ap().rearrange("p (tt e) -> p tt e", e=E)[:, 0:NT2, :]

    def pa_wait(bk):
        nm, v = pa_free[bk]
        if nm == "ac":
            s.wait("tensor", s_ac, v)
        elif nm == "dv":
            s.wait("tensor", s_dv, v)

    def ph1_gen(c):
        ci = c % (NC1 // SEQ)
        x_tm = x_tm2[c % 2]
        if c >= 2:
            s.wait("sync", s_dv, Vend_hist[c - 2])
        s.dma("sync", x_tm[:], x_r[c], s_ld)
        L_x = s_ld.n

        if ci == 0 and c > 0:
            s.op("gpsimd", lambda g: g.memset(carry_r[:], 0.0))
            s.op("gpsimd", lambda g: g.memset(carry_i[:], 0.0))
            gflush(s_gp)

        s.wait("vector", s_ld, L_x)
        s.wait("vector", s_ld, p1["L_t_prev"])
        s.op("vector", lambda v: v.reduce_sum(lnc[:, 6:7], x_tm[:], AX.X), inc=s_dv)
        s.wait("scalar", s_ld, L_x)
        s.wait("scalar", s_gp, p1["G_carry"])
        s.op("scalar", lambda sc: sc.activation(sqv, x_tm[:], AF.Square), inc=s_ac)
        A_sq = s_ac.n
        s.wait("vector", s_ac, A_sq)
        s.op("vector", lambda v: v.reduce_sum(s2c[:], sqv, AX.X))
        s.op("vector", lambda v: v.tensor_single_scalar(lnc[:, 2:3], lnc[:, 6:7],
                                                        1.0 / D, ALU.mult))
        s.op("vector", lambda v: v.tensor_single_scalar(lnc[:, 5:6], s2c[:],
                                                        1.0 / D, ALU.mult))
        s.op("vector", lambda v: v.tensor_mul(lnc[:, 3:4], lnc[:, 2:3], lnc[:, 2:3]))
        vsp()
        s.op("vector", lambda v: v.tensor_sub(lnc[:, 1:2], lnc[:, 5:6], lnc[:, 3:4]))
        vflush(s_dv)
        V_var = s_dv.n
        s.wait("scalar", s_dv, V_var)
        s.op("scalar", lambda sc: sc.activation(lnc[:, 4:5], lnc[:, 1:2], AF.Ln,
                                                bias=ccol(C_EPS)))
        s.op("scalar", lambda sc: sc.copy(bar[:, 4:5], cc[:, 0:1]))
        s.op("scalar", lambda sc: sc.activation(lnc[:, 0:1], lnc[:, 4:5], AF.Exp,
                                                scale=ccol(C_NEGHALF)))
        aflush(s_ac)
        A_rstd = s_ac.n
        s.wait("vector", s_ac, A_rstd)
        s.op("vector", lambda v: v.tensor_scalar(
            xn_tm[:], x_tm[:], lnc[:, 2:3], lnc[:, 0:1], ALU.subtract, ALU.mult),
            inc=s_dv)
        V_ln = s_dv.n

        yield
        buf = (c // 4) % 2
        q = c % 4
        s.wait("sync", s_dv, V_ln)
        if p2["xn2_freeby"][buf]:
            s.wait("sync", s_pe, p2["xn2_freeby"][buf])
        s.dmat("sync", xn2[buf][:, :, q * 128:(q + 1) * 128], xn_tm[:], s_ld)
        L_t = s_ld.n
        p1["L_t_prev"] = L_t
        S_xnt[c] = L_t

        if c == 0:
            s.wait("tensor", s_gw, GW_ATTN)
        s.wait("tensor", s_ld, L_t)
        for ht in range(16):
            bk = p1["mm_rot"] % 2
            p1["mm_rot"] += 1
            pa_wait(bk)
            for kt in range(8):
                s.op("tensor", lambda t, ht=ht, kt=kt, bk=bk, buf=buf, q=q: t.matmul(
                    p_a[bk][:, 0:L1], w_in_sb[:, kt, ht * 128:(ht + 1) * 128],
                    xn2[buf][:, kt, q * 128:(q + 1) * 128],
                    start=(kt == 0), stop=(kt == 7)),
                    inc=(s_pe if kt == 7 else None))
            P_ht = s_pe.n
            s.wait("vector", s_pe, P_ht)
            if ht == 0:
                s.wait("vector", s_gp, p1.get("G_rot", 0))
            s.op("vector", lambda v, ht=ht, bk=bk: v.tensor_scalar(
                xh_f[:, ht, :], p_a[bk][:, 0:L1], bin_c[:, ht:ht + 1], None,
                ALU.add), inc=s_dv)
            V_xh = s_dv.n
            if ht == 0:
                s.wait("scalar", s_pe, p1["P_bdl_prev"])
                s.wait("scalar", s_gp, p1["G_carry"])
            s.wait("scalar", s_dv, V_xh)
            s.op("scalar", lambda sc, ht=ht, bk=bk: sc.activation(
                xh_b[:, ht, :], p_a[bk][:, 0:L1], AF.Identity,
                bias=bin_c[:, ht:ht + 1]), inc=s_ac)
            pa_free[bk] = ("ac", s_ac.n)

        yield
        s.wait("scalar", s_dv, p1["V_end"])
        s.wait("scalar", s_gp, p1["G_carry"])
        for wsb, bcol, gpl in ((wgx_sb, bgx_c, GX), (wga_sb, bga_c, GA)):
            for b in range(NB):
                bk = p1["mm_rot"] % 2
                p1["mm_rot"] += 1
                pa_wait(bk)
                for kt in range(2):
                    s.op("tensor", lambda t, wsb=wsb, b=b, kt=kt, bk=bk: t.matmul(
                        p_a[bk][:, 0:L1], wsb[:, b, kt, :],
                        xh_b[:, b * 2 + kt, :], start=(kt == 0), stop=(kt == 1)),
                        inc=(s_pe if kt == 1 else None))
                P_g = s_pe.n
                s.wait("scalar", s_pe, P_g)
                s.op("scalar", lambda sc, gpl=gpl, b=b, bk=bk, bcol=bcol: sc.activation(
                    gpl[:, b, :], p_a[bk][:, 0:L1], AF.Sigmoid,
                    bias=bcol[:, b:b + 1]), inc=s_ac)
                pa_free[bk] = ("ac", s_ac.n)
        p1["P_bdl_prev"] = s_pe.n
        A_gates = s_ac.n

        yield
        if c == 0:
            s.wait("vector", s_dv, max(V_MSP, V_INIT))
        s.wait("vector", s_ac, A_gates)
        s.op("vector", lambda v: v.tensor_tensor(T0[:], GA[:], mspb, ALU.mult))
        s.op("vector", lambda v: v.tensor_tensor(T1[:], GA[:], aipb, ALU.mult),
             inc=s_dv)
        for ct in range(8):
            s.op("vector", lambda v, ct=ct: v.tensor_tensor_scan(
                T2[:, ct, :], ones_pl[:], T1[:, ct, :], 0.0, ALU.mult, ALU.add))
        s.op("vector", lambda v: v.tensor_single_scalar(
            T1[:], T2[:], 1.0 / (2 * PI), ALU.mult))
        s.wait("vector", s_gp, p1["G_carry"])
        s.op("vector", lambda v: v.tensor_copy(KIap[:], T1[:]))
        s.op("vector", lambda v: v.tensor_copy(T1[:], KIap[:]))
        s.op("vector", lambda v: v.scalar_tensor_tensor(
            T1[:], T1[:], -2 * PI, T2[:], ALU.mult, ALU.add), inc=s_dv)
        V_phi = s_dv.n
        yield
        s.wait("scalar", s_dv, V_phi)
        s.op("scalar", lambda sc: sc.activation(T2[:], T0[:], AF.Exp))
        s.op("scalar", lambda sc: sc.activation(T0[:], T2[:], AF.Square), inc=s_ac)
        A_q = s_ac.n
        s.wait("vector", s_ac, A_q)
        s.op("vector", lambda v: v.tensor_scalar(
            T0[:], T0[:], -1.0, 1.0, ALU.mult, ALU.add))
        s.op("vector", lambda v: v.tensor_single_scalar(
            T0[:], T0[:], 1e-6, ALU.max), inc=s_dv)
        V_r2 = s_dv.n
        s.wait("scalar", s_dv, V_r2)
        s.op("scalar", lambda sc: sc.activation(T3[:], T0[:], AF.Ln))
        s.op("scalar", lambda sc: sc.activation(T0[:], T3[:], AF.Exp,
                                                scale=ccol(C_HALF)), inc=s_ac)
        s.op("scalar", lambda sc: sc.activation(T3[:], T1[:], AF.Sin))
        s.op("scalar", lambda sc: sc.activation(T5[:], T1[:], AF.Abs))
        s.op("scalar", lambda sc: sc.activation(
            T4[:], T5[:], AF.Sin, scale=ccol(C_NEG1), bias=ccol(C_PIH)), inc=s_ac)
        A_trig = s_ac.n
        yield
        s.wait("gpsimd", s_ac, A_trig)
        s.wait("gpsimd", s_dv, V_phi)
        s.op("gpsimd", lambda g: g.tensor_mul(T1[:], GX[:], T0[:]))
        s.op("gpsimd", lambda g: g.tensor_mul(GX[:], T4[:], T1[:]))
        s.op("gpsimd", lambda g: g.tensor_mul(T0[:], T3[:], T1[:]))
        s.op("gpsimd", lambda g: g.tensor_mul(T1[:], GX[:], xr))
        s.op("gpsimd", lambda g: g.tensor_mul(T5[:], T0[:], xi))
        s.op("gpsimd", lambda g: g.tensor_add(T1[:], T1[:], T5[:]))
        s.op("gpsimd", lambda g: g.tensor_mul(GA[:], GX[:], xi))
        s.op("gpsimd", lambda g: g.tensor_mul(T5[:], T0[:], xr))
        s.op("gpsimd", lambda g: g.tensor_sub(GA[:], GA[:], T5[:]))
        gflush(s_gp)
        G_rot = s_gp.n
        p1["G_rot"] = G_rot
        yield
        s.wait("vector", s_gp, G_rot)
        for ct in range(8):
            s.op("vector", lambda v, ct=ct: v.tensor_tensor_scan(
                GX[:, ct, :], T2[:, ct, :], T1[:, ct, :], carry_r[:, ct:ct + 1],
                ALU.mult, ALU.add))
        for ct in range(8):
            s.op("vector", lambda v, ct=ct: v.tensor_tensor_scan(
                T0[:, ct, :], T2[:, ct, :], GA[:, ct, :], carry_i[:, ct:ct + 1],
                ALU.mult, ALU.add))
        vflush(s_dv)
        V_scans = s_dv.n
        s.wait("gpsimd", s_dv, V_scans)
        s.wait("gpsimd", s_pe, p1["P_wo_prev"])
        s.op("gpsimd", lambda g: g.tensor_mul(GA[:], T4[:], GX[:]))
        s.op("gpsimd", lambda g: g.tensor_mul(T5[:], T3[:], T0[:]))
        s.op("gpsimd", lambda g: g.tensor_sub(hr, GA[:], T5[:]))
        s.op("gpsimd", lambda g: g.tensor_mul(GA[:], T3[:], GX[:]))
        s.op("gpsimd", lambda g: g.tensor_mul(T5[:], T4[:], T0[:]))
        s.op("gpsimd", lambda g: g.tensor_add(hi, GA[:], T5[:]))
        LC = L1 - 1
        s.op("gpsimd", lambda g: g.tensor_mul(tc0[:], T4[:, :, LC], GX[:, :, LC]))
        s.op("gpsimd", lambda g: g.tensor_mul(tc1[:], T3[:, :, LC], T0[:, :, LC]))
        s.op("gpsimd", lambda g: g.tensor_mul(tc2[:], T3[:, :, LC], GX[:, :, LC]))
        s.op("gpsimd", lambda g: g.tensor_mul(tc3[:], T4[:, :, LC], T0[:, :, LC]))
        s.op("gpsimd", lambda g: g.tensor_sub(carry_r[:], tc0[:], tc1[:]))
        s.op("gpsimd", lambda g: g.tensor_add(carry_i[:], tc2[:], tc3[:]))
        gflush(s_gp)
        p1["G_carry"] = s_gp.n
        V_h = s_gp.n
        yield
        yield
        s.wait("tensor", s_gp, V_h)
        pa_wait(0)
        pa_wait(1)
        for kt in range(16):
            for nh in range(2):
                s.op("tensor", lambda t, nh=nh, kt=kt: t.matmul(
                    p_a[nh][:, 0:512], h_bf[:, kt, :],
                    w_out_sb[:, kt, nh * 512:(nh + 1) * 512],
                    start=(kt == 0), stop=(kt == 15)),
                    inc=(s_pe if (nh == 1 and kt == 15) else None))
        P_wo = s_pe.n
        p1["P_wo_prev"] = P_wo
        s.wait("vector", s_pe, P_wo)
        if p1["S_p_prev"]:
            s.wait("vector", s_st, p1["S_p_prev"])
        for nh in range(2):
            sl = slice(nh * 512, (nh + 1) * 512)
            s.op("vector", lambda v, nh=nh, sl=sl: v.scalar_tensor_tensor(
                partial[:, sl], p_a[nh][:, 0:512], 1.0, x_tm[:, sl],
                ALU.mult, ALU.add))
            s.op("vector", lambda v, sl=sl: v.tensor_add(
                partial[:, sl], partial[:, sl], bout_pl[:, sl]), inc=s_dv)
        pa_free[0] = ("dv", s_dv.n)
        pa_free[1] = ("dv", s_dv.n)
        p1["V_end"] = s_dv.n
        Vend_hist[c] = s_dv.n

        s.wait("sync", s_dv, p1["V_end"])
        s.dma("sync", pscr_h[c], partial[:], s_st)
        p1["S_p_prev"] = s_st.n
        S_pscr[c] = s_st.n

    def ph2_gen(c2):
        buf = c2 % 2
        if p2["S_part_prev"]:
            s.wait("sync", s_st, p2["S_part_prev"])
        s.wait("sync", s_st, S_pscr[4 * c2 + 3])
        for q in range(4):
            s.dma("sync", part2[:, q, :], pscr_h[4 * c2 + q], s_p2l)
        p2["P2L"] = s_p2l.n

        s.wait("tensor", s_ld, S_xnt[4 * c2 + 3])
        for po in range(2):
            if p2["po_free"][po]:
                s.wait("tensor", s_dv, p2["po_free"][po])
        for tt in range(NT2):
            for kt in range(8):
                s.op("tensor", lambda t, tt=tt, kt=kt, buf=buf: t.matmul(
                    p_o0[:, tt * E:(tt + 1) * E],
                    xn2[buf][:, kt, tt * 128:(tt + 1) * 128],
                    rw_sb[:, kt, :], start=(kt == 0), stop=(kt == 7)),
                    inc=(s_pe if (tt == NT2 - 1 and kt == 7) else None))
        P_r = s_pe.n
        s.wait("vector", s_pe, P_r)
        s.op("vector", lambda v: v.tensor_reduce(
            sm0[:], prv, AX.X, ALU.max, negate=True))
        vflush(s_dv)
        V_nm = s_dv.n
        s.wait("scalar", s_dv, V_nm)
        for tt in range(NT2):
            s.op("scalar", lambda sc, tt=tt: sc.activation(
                pr4[:, tt, :], prv[:, tt, :], AF.Exp, bias=sm0[:, tt:tt + 1]))
        aflush(s_ac)
        A_exp = s_ac.n
        s.wait("vector", s_ac, A_exp)
        s.op("vector", lambda v: v.reduce_sum(sm1[:], pr4[:], AX.X), inc=s_dv)
        p2["po_free"][0] = s_dv.n
        vsp()
        s.op("vector", lambda v: v.reciprocal(sm0[:], sm1[:]))
        vsp()
        s.op("vector", lambda v: v.tensor_tensor(
            pr4[:], pr4[:], sm0.ap().broadcast_to([128, NT2, E]), ALU.mult))
        vsp()
        s.op("vector", lambda v: v.tensor_reduce(sm1[:], pr4[:], AX.X, ALU.max))
        vsp()
        s.op("vector", lambda v: v.tensor_tensor(
            lp[:], pr4[:], sm1.ap().broadcast_to([128, NT2, E]), ALU.is_lt))
        vsp()
        s.op("vector", lambda v: v.tensor_mul(lp[:], lp[:], pr4[:]))
        vsp()
        s.op("vector", lambda v: v.tensor_reduce(sm1[:], lp[:], AX.X, ALU.max))
        vsp()
        s.op("vector", lambda v: v.tensor_tensor(
            lp[:], pr4[:], sm1.ap().broadcast_to([128, NT2, E]), ALU.is_ge))
        vsp()
        s.op("vector", lambda v: v.tensor_mul(lp[:], lp[:], pr4[:]))
        vflush(s_dv)
        p2["V_lp"] = s_dv.n
        yield
        if debug and c2 == dbgc2:
            s.wait("sync", s_dv, p2["V_lp"])
            s.dma("sync", dbg_lp, lp[:], s_ld)
            s.dma("sync", dbg_pr4, pr4[:], s_ld)
            s.dma("sync", dbg_sm, sm0[:], s_ld)
            s.dma("sync", dbg_xn2, xn2[c2 % 2][:], s_ld)
            s.dma("sync", dbg_part2, part2[:], s_ld)
            s.wait("vector", s_ld)
            s.wait("scalar", s_ld)
        for e in range(E):
            for _u in ph2_expert(c2, e):
                yield
        V_acc = s_dv.n
        s.wait("sync", s_dv, V_acc)
        s.dma("sync", y_r[c2], part2[:], s_st)
        p2["S_part_prev"] = s_st.n

    def ph2_expert(c2, e):
        buf = c2 % 2
        if c2 == 0:
            s.wait("sync", s_gw, GW_CONV_E[e])
        G_hh = []
        for hh in range(2):
            wslot = p2["ghh"] % W2R_
            p2["ghh"] += 1
            if p2["w2_free"][wslot]:
                s.wait("sync", s_pe, p2["w2_free"][wslot])
            s.dma("sync", w2r[:, wslot], w2bf_h[e, hh], s_w2s[wslot])
            G_hh.append((wslot, s_w2s[wslot].n))

        if e == 0:
            s.wait("tensor", s_dv, p2["V_lp"])
        for ft in range(11):
            slot = p2["gft"] % W1R_
            p2["gft"] += 1
            if p2["w1_free"][slot]:
                s.wait("sync", s_pe, p2["w1_free"][slot])
            s.dma("sync", w1r[:, slot], w1bf_h[e, ft], s_w1s[slot])
            gval = s_w1s[slot].n
            par = ft % 2
            s.wait("tensor", s_w1s[slot], gval)
            if p2["py_free"][par]:
                s.wait("tensor", s_dv, p2["py_free"][par])
            for half in range(2):
                for kt in range(8):
                    s.op("tensor", lambda t, slot=slot, half=half, kt=kt, par=par,
                         buf=buf: t.matmul(
                        p_y[par][half][:],
                        w1r[:, slot, half, kt, :],
                        xn2[buf][:, kt, :],
                        start=(kt == 0), stop=(kt == 7)),
                        inc=(s_pe if (half == 1 and kt == 7) else None))
            P_yft = s_pe.n
            p2["w1_free"][slot] = P_yft
            s.wait("scalar", s_pe, P_yft)
            if p2["py_free"][par ^ 1]:
                s.wait("scalar", s_dv, p2["py_free"][par ^ 1])
            s.op("scalar", lambda sc, par=par: sc.activation(
                silu_sb[par][:], p_y[par][0][:], AF.Silu), inc=s_ac)
            A_sil = s_ac.n
            s.wait("vector", s_ac, A_sil)
            s.op("vector", lambda v, ft=ft, par=par: v.tensor_mul(
                gbuf[:, ft, :], silu_sb[par][:], p_y[par][1][:]), inc=s_dv)
            p2["py_free"][par] = s_dv.n
            yield
        V_g = s_dv.n
        if debug and c2 == dbgc2 and e == 0:
            s.wait("sync", s_dv, V_g)
            s.dma("sync", dbg_g, gbuf[:], s_ld)
            s.wait("tensor", s_ld)
        if e == E - 1:
            p2["xn2_freeby"][buf] = s_pe.n

        s.wait("tensor", s_dv, V_g)
        if e == 0:
            s.wait("vector", s_p2l, p2["P2L"])
        for hh in range(2):
            wslot, gval = G_hh[hh]
            s.wait("tensor", s_w2s[wslot], gval)
            for tt in range(NT2):
                po = p2["gtt"] % 2
                p2["gtt"] += 1
                if p2["po_free"][po]:
                    s.wait("tensor", s_dv, p2["po_free"][po])
                for ft in range(11):
                    s.op("tensor", lambda t, tt=tt, ft=ft, po=po, wslot=wslot: t.matmul(
                        p_o[po][:], gbuf[:, ft, tt * 128:(tt + 1) * 128],
                        w2r[:, wslot, ft, :],
                        start=(ft == 0), stop=(ft == 10)),
                        inc=(s_pe if ft == 10 else None))
                P_oms = s_pe.n
                s.wait("vector", s_pe, P_oms)
                sl = slice(hh * 512, (hh + 1) * 512)
                s.op("vector", lambda v, tt=tt, po=po, e=e, sl=sl: v.scalar_tensor_tensor(
                    part2[:, tt, sl], p_o[po][:], lp[:, tt, e:e + 1],
                    part2[:, tt, sl], ALU.mult, ALU.add), inc=s_dv)
                p2["po_free"][po] = s_dv.n
                yield
            p2["w2_free"][wslot] = s_pe.n
        if debug and c2 == dbgc2 and e == 0:
            s.wait("sync", s_dv, s_dv.n)
            s.dma("sync", dbg_p2, part2[:], s_ld)
            s.wait("tensor", s_ld)
            s.wait("vector", s_ld)

    ph2q = []
    next_c2 = 0

    def pump_ph2(n):
        emitted = 0
        while emitted < n and ph2q:
            try:
                next(ph2q[0][1])
                emitted += 1
            except StopIteration:
                ph2q.pop(0)

    for k in range(NC1):
        while ph2q and ph2q[0][0] <= k // 4 - 2:
            try:
                next(ph2q[0][1])
            except StopIteration:
                ph2q.pop(0)
        for _u in ph1_gen(k):
            pump_ph2(2)
        if next_c2 < NC2 and k >= 4 * next_c2 + PH2_LAG - 1:
            ph2q.append((next_c2, ph2_gen(next_c2)))
            next_c2 += 1
        pump_ph2(2)
    while next_c2 < NC2:
        ph2q.append((next_c2, ph2_gen(next_c2)))
        next_c2 += 1
    while ph2q:
        try:
            next(ph2q[0][1])
        except StopIteration:
            ph2q.pop(0)

    s.wait("sync", s_st, s_st.n)

    s.emit(nc, "mk")
    return nc


def make_in_maps(L0_attn_x, L0_attn_w_in, L0_attn_b_in, L0_attn_w_gx, L0_attn_b_gx,
                 L0_attn_w_ga, L0_attn_b_ga, L0_attn_a_real_param, L0_attn_a_imag_param,
                 L0_attn_w_out, L0_attn_b_out, L0_ffn_router_weight, L0_ffn_w1,
                 L0_ffn_w2):
    f32 = np.float32
    cc = np.zeros((128, 8), f32)
    cc[:, C_ONE] = 1.0
    cc[:, C_NEG1] = -1.0
    cc[:, C_HALF] = 0.5
    cc[:, C_NEGHALF] = -0.5
    cc[:, C_PIH] = PI / 2
    cc[:, C_EPS] = 1e-5
    cc[:, C_NEG2PI] = -2 * PI
    cc[:, C_I2PI] = 1.0 / (2 * PI)
    shared = dict(
        w_in=np.ascontiguousarray(L0_attn_w_in, f32),
        b_in=np.ascontiguousarray(np.asarray(L0_attn_b_in, f32).reshape(16, 128).T),
        w_gx=np.ascontiguousarray(L0_attn_w_gx, f32),
        b_gx=np.ascontiguousarray(np.asarray(L0_attn_b_gx, f32).reshape(8, 128).T),
        w_ga=np.ascontiguousarray(L0_attn_w_ga, f32),
        b_ga=np.ascontiguousarray(np.asarray(L0_attn_b_ga, f32).reshape(8, 128).T),
        arp=np.ascontiguousarray(np.asarray(L0_attn_a_real_param, f32).reshape(8, 128).T),
        aip=np.ascontiguousarray(np.asarray(L0_attn_a_imag_param, f32).reshape(8, 128).T),
        w_out=np.ascontiguousarray(L0_attn_w_out, f32),
        bout_pl=np.ascontiguousarray(
            np.broadcast_to(np.asarray(L0_attn_b_out, f32), (128, D))),
        rw=np.ascontiguousarray(L0_ffn_router_weight, f32),
        w1=np.ascontiguousarray(L0_ffn_w1, f32),
        w2=np.ascontiguousarray(L0_ffn_w2, f32),
        cc=cc,
    )
    x = np.ascontiguousarray(np.asarray(L0_attn_x, f32))
    in_maps = []
    for i in range(8):
        m = dict(shared)
        m["x"] = np.ascontiguousarray(x[2 * i:2 * i + 2])
        in_maps.append(m)
    return in_maps


def kernel(**inputs):
    in_maps = make_in_maps(**inputs)
    nc = build_nc()
    res = run_bass_kernel_spmd(nc, in_maps, core_ids=list(range(8)))
    return np.concatenate([np.asarray(r["y"], np.float32) for r in res.results], axis=0)


# version 09
# speedup vs baseline: 1.1087x; 1.1087x over previous
import numpy as np
import ml_dtypes
import concourse.bass as bass
import concourse.mybir as mybir
import concourse.bass_utils as _bass_utils
from concourse.bass_utils import run_bass_kernel_spmd

if not getattr(_bass_utils, "_ldw_opt_patched", False):
    _orig_run_command = _bass_utils.run_command

    def _run_command_ldw(cmd, *a, **kw):
        if isinstance(cmd, list):
            cmd = ["--enable-ldw-opt=true" if c == "--enable-ldw-opt=false" else c
                   for c in cmd]
        return _orig_run_command(cmd, *a, **kw)

    _bass_utils.run_command = _run_command_ldw
    _bass_utils._ldw_opt_patched = True

F32 = mybir.dt.float32
BF16 = mybir.dt.bfloat16
I32 = mybir.dt.int32
AF = mybir.ActivationFunctionType
ALU = mybir.AluOpType
AX = mybir.AxisListType

PI = float(np.pi)

SEQ = 2
T = 4096
D = 1024
DH = 2048
NB = 8
E = 8
FF = 1408
L1 = 128
NC1 = SEQ * T // L1
L2 = 512
NT2 = L2 // 128
NC2 = SEQ * T // L2
PH2_LAG = 5

C_ONE, C_NEG1, C_HALF, C_NEGHALF, C_PIH, C_EPS, C_NEG2PI, C_I2PI = range(8)

ENGS = ("sync", "gpsimd", "tensor", "vector", "scalar")


class Ctr:
    def __init__(self, nc, name):
        self.sem = nc.alloc_semaphore(name)
        self.n = 0


class Sched:
    def __init__(self):
        self.q = {e: [] for e in ENGS}

    def on(self, eng, fn):
        self.q[eng].append(fn)

    def wait(self, eng, ctr, val=None):
        v = ctr.n if val is None else val
        if v <= 0:
            return
        sem = ctr.sem
        self.on(eng, lambda e, s=sem, vv=v: e.wait_ge(s, vv))

    def op(self, eng, fn, inc=None, k=1):
        if inc is not None:
            inc.n += k
            sem = inc.sem
            self.on(eng, lambda e, f=fn, s=sem, kk=k: f(e).then_inc(s, kk))
        else:
            self.on(eng, fn)

    def dma(self, eng, out, in_, inc):
        inc.n += 16
        sem = inc.sem
        self.on(eng, lambda e, o=out, i=in_, s=sem: e.dma_start(out=o, in_=i).then_inc(s, 16))

    def dmat(self, eng, out, in_, inc):
        inc.n += 16
        sem = inc.sem
        self.on(eng, lambda e, o=out, i=in_, s=sem: e.dma_start_transpose(o, i).then_inc(s, 16))

    def emit(self, nc, name):
        with nc.Block(name) as block:
            @block.sync
            def _(e):
                for fn in self.q["sync"]:
                    fn(e)

            @block.gpsimd
            def _(e):
                for fn in self.q["gpsimd"]:
                    fn(e)

            @block.tensor
            def _(e):
                for fn in self.q["tensor"]:
                    fn(e)

            @block.vector
            def _(e):
                for fn in self.q["vector"]:
                    fn(e)

            @block.scalar
            def _(e):
                for fn in self.q["scalar"]:
                    fn(e)


def build_nc(debug=False, dbgc2=0):
    nc = bass.Bass("TRN2", target_bir_lowering=False)

    x_h = nc.dram_tensor("x", [SEQ, T, D], F32, kind="ExternalInput").ap()
    w_in_h = nc.dram_tensor("w_in", [D, DH], F32, kind="ExternalInput").ap()
    b_in_h = nc.dram_tensor("b_in", [128, 16], F32, kind="ExternalInput").ap()
    w_gx_h = nc.dram_tensor("w_gx", [NB, 256, 128], F32, kind="ExternalInput").ap()
    b_gx_h = nc.dram_tensor("b_gx", [128, 8], F32, kind="ExternalInput").ap()
    w_ga_h = nc.dram_tensor("w_ga", [NB, 256, 128], F32, kind="ExternalInput").ap()
    b_ga_h = nc.dram_tensor("b_ga", [128, 8], F32, kind="ExternalInput").ap()
    arp_h = nc.dram_tensor("arp", [128, 8], F32, kind="ExternalInput").ap()
    aip_h = nc.dram_tensor("aip", [128, 8], F32, kind="ExternalInput").ap()
    w_out_h = nc.dram_tensor("w_out", [DH, D], F32, kind="ExternalInput").ap()
    bout_pl_h = nc.dram_tensor("bout_pl", [128, D], F32, kind="ExternalInput").ap()
    rw_h = nc.dram_tensor("rw", [D, E], F32, kind="ExternalInput").ap()
    w1_h = nc.dram_tensor("w1", [E, D, 2 * FF], F32, kind="ExternalInput").ap()
    w2_h = nc.dram_tensor("w2", [E, FF, D], F32, kind="ExternalInput").ap()
    cc_h = nc.dram_tensor("cc", [128, 8], F32, kind="ExternalInput").ap()

    y_h = nc.dram_tensor("y", [SEQ, T, D], F32, kind="ExternalOutput").ap()

    _scr = "ExternalOutput" if debug else "Internal"
    w1bf_h = nc.dram_tensor("w1bf", [E, 11, 128, 2, 8, 128], BF16, kind=_scr).ap()
    w2bf_h = nc.dram_tensor("w2bf", [E, 2, 128, 11, 512], BF16, kind=_scr).ap()
    pscr_h = nc.dram_tensor("pscr", [NC1, 128, D], F32, kind=_scr).ap()
    if debug:
        dbg_lp = nc.dram_tensor("dbg_lp", [128, NT2, E], F32, kind="ExternalOutput").ap()
        dbg_pr4 = nc.dram_tensor("dbg_pr4", [128, NT2, E], F32, kind="ExternalOutput").ap()
        dbg_sm = nc.dram_tensor("dbg_sm", [128, NT2], F32, kind="ExternalOutput").ap()
        dbg_xn2 = nc.dram_tensor("dbg_xn2", [128, 8, L2], BF16, kind="ExternalOutput").ap()
        dbg_part2 = nc.dram_tensor("dbg_part2", [128, NT2, D], F32, kind="ExternalOutput").ap()
        dbg_g = nc.dram_tensor("dbg_g", [128, 11, L2], BF16, kind="ExternalOutput").ap()
        dbg_p2 = nc.dram_tensor("dbg_p2", [128, NT2, D], F32, kind="ExternalOutput").ap()

    cc = nc.alloc_sbuf_tensor("cc_sb", [128, 8], F32)
    bout_pl = nc.alloc_sbuf_tensor("bout_sb", [128, D], F32)
    rw_sb = nc.alloc_sbuf_tensor("rw_sb", [128, 8, E], BF16)
    bin_c = nc.alloc_sbuf_tensor("bin_c", [128, 16], F32)
    bgx_c = nc.alloc_sbuf_tensor("bgx_c", [128, 8], F32)
    bga_c = nc.alloc_sbuf_tensor("bga_c", [128, 8], F32)
    arp_c = nc.alloc_sbuf_tensor("arp_c", [128, 8], F32)
    aip_c = nc.alloc_sbuf_tensor("aip_c", [128, 8], F32)
    msp_c = nc.alloc_sbuf_tensor("msp_c", [128, 8], F32)
    tc0 = nc.alloc_sbuf_tensor("tc0", [128, 8], F32)
    tc1 = nc.alloc_sbuf_tensor("tc1", [128, 8], F32)
    tc2 = nc.alloc_sbuf_tensor("tc2", [128, 8], F32)
    tc3 = nc.alloc_sbuf_tensor("tc3", [128, 8], F32)
    bar = nc.alloc_sbuf_tensor("bar", [128, 8], F32)
    ones_pl = nc.alloc_sbuf_tensor("ones_pl", [128, L1], F32)
    lnc = nc.alloc_sbuf_tensor("lnc", [128, 8], F32)
    s2c = nc.alloc_sbuf_tensor("s2c", [128, 1], F32)
    carry_r = nc.alloc_sbuf_tensor("carry_r", [128, 8], F32)
    carry_i = nc.alloc_sbuf_tensor("carry_i", [128, 8], F32)

    w_in_sb = nc.alloc_sbuf_tensor("w_in_sb", [128, 8, DH], BF16)
    wgx_sb = nc.alloc_sbuf_tensor("wgx_sb", [128, NB, 2, 128], BF16)
    wga_sb = nc.alloc_sbuf_tensor("wga_sb", [128, NB, 2, 128], BF16)
    w_out_sb = nc.alloc_sbuf_tensor("w_out_sb", [128, 16, D], BF16)
    x_tm2 = [nc.alloc_sbuf_tensor(f"x_tm{i}", [128, D], F32) for i in range(2)]
    xn_tm = nc.alloc_sbuf_tensor("xn_tm", [128, D], BF16)
    xh_f = nc.alloc_sbuf_tensor("xh_f", [128, 16, L1], F32)
    GA = nc.alloc_sbuf_tensor("GA", [128, 8, L1], F32)
    GX = nc.alloc_sbuf_tensor("GX", [128, 8, L1], F32)
    T0 = nc.alloc_sbuf_tensor("T0", [128, 8, L1], F32)
    T1 = nc.alloc_sbuf_tensor("T1", [128, 8, L1], F32)
    T2 = nc.alloc_sbuf_tensor("T2", [128, 8, L1], F32)
    T3 = nc.alloc_sbuf_tensor("T3", [128, 8, L1], F32)
    T4 = nc.alloc_sbuf_tensor("T4", [128, 8, L1], F32)
    T5 = nc.alloc_sbuf_tensor("T5", [128, 8, L1], F32)
    partial = nc.alloc_sbuf_tensor("partial", [128, D], F32)

    KIap = T5.ap().bitcast(I32)
    h_bf = T1.ap().bitcast(BF16).rearrange("p c (x t) -> p (c x) t", x=2)
    xh_b = T3.ap().bitcast(BF16).rearrange("p c (x t) -> p (c x) t", x=2)
    sqv = T0.ap().rearrange("p a b -> p (a b)")

    W1R = 2
    w1r = nc.alloc_sbuf_tensor("w1r", [128, W1R, 2, 8, 128], BF16)
    W2R = 2
    w2r = nc.alloc_sbuf_tensor("w2r", [128, W2R, 11, 512], BF16)
    gbuf = nc.alloc_sbuf_tensor("gbuf", [128, 11, L2], BF16)
    silu_sb = [nc.alloc_sbuf_tensor("silu_0", [128, L2], F32)] * 2
    xn2 = [nc.alloc_sbuf_tensor(f"xn2_{i}", [128, 8, L2], BF16) for i in range(2)]
    part2 = nc.alloc_sbuf_tensor("part2", [128, NT2, D], F32)
    lp = nc.alloc_sbuf_tensor("lp", [128, NT2, E], F32)
    pr4 = nc.alloc_sbuf_tensor("pr4", [128, NT2, E], F32)
    sm0 = nc.alloc_sbuf_tensor("sm0", [128, NT2], F32)
    sm1 = nc.alloc_sbuf_tensor("sm1", [128, NT2], F32)

    p_a0 = nc.alloc_psum_tensor("p_a0", [128, 512], F32)
    p_a1 = nc.alloc_psum_tensor("p_a1", [128, 512], F32)
    p_y00 = nc.alloc_psum_tensor("p_y00", [128, 512], F32)
    p_y01 = nc.alloc_psum_tensor("p_y01", [128, 512], F32)
    p_y10 = nc.alloc_psum_tensor("p_y10", [128, 512], F32)
    p_y11 = nc.alloc_psum_tensor("p_y11", [128, 512], F32)
    p_o0 = nc.alloc_psum_tensor("p_o0", [128, 512], F32)
    p_o1 = nc.alloc_psum_tensor("p_o1", [128, 512], F32)
    p_a = [p_a0, p_a1]
    p_y = [[p_y00, p_y01], [p_y10, p_y11]]
    p_o = [p_o0, p_o1]

    s_ld = Ctr(nc, "s_ld")
    s_gw = Ctr(nc, "s_gw")
    s_pe = Ctr(nc, "s_pe")
    s_dv = Ctr(nc, "s_dv")
    s_ac = Ctr(nc, "s_ac")
    s_st = Ctr(nc, "s_st")
    s_w1s = [Ctr(nc, "s_w1s0"), Ctr(nc, "s_w1s1")]
    s_w2s = [Ctr(nc, "s_w2s0"), Ctr(nc, "s_w2s1")]
    s_p2l = Ctr(nc, "s_p2l")
    s_gp = Ctr(nc, "s_gp")

    def ccol(i):
        return cc[:, i:i + 1]

    s = Sched()

    def vflush(inc):
        s.op("vector", lambda v: v.tensor_copy(bar[:, 2:3], cc[:, 0:1]), inc=inc)

    def aflush(inc):
        s.op("scalar", lambda sc: sc.copy(bar[:, 3:4], cc[:, 0:1]), inc=inc)

    def vsp():
        s.op("vector", lambda v: v.tensor_copy(bar[:, 5:6], cc[:, 1:2]))

    def gflush(inc):
        s.op("gpsimd", lambda g: g.tensor_copy(bar[:, 6:7], cc[:, 0:1]), inc=inc)

    s.dma("sync", cc[:], cc_h, s_ld)
    s.dma("sync", bout_pl[:], bout_pl_h, s_ld)
    s.dma("sync", bin_c[:], b_in_h, s_ld)
    s.dma("sync", bgx_c[:], b_gx_h, s_ld)
    s.dma("sync", bga_c[:], b_ga_h, s_ld)
    s.dma("sync", arp_c[:], arp_h, s_ld)
    s.dma("sync", aip_c[:], aip_h, s_ld)
    LD_CONSTS = s_ld.n

    s.dma("gpsimd", w_in_sb[:], w_in_h.rearrange("(k p) h -> p k h", p=128), s_gw)
    s.dma("gpsimd", wgx_sb[:], w_gx_h.rearrange("b (kt p) o -> p b kt o", p=128), s_gw)
    s.dma("gpsimd", wga_sb[:], w_ga_h.rearrange("b (kt p) o -> p b kt o", p=128), s_gw)
    s.dma("gpsimd", w_out_sb[:], w_out_h.rearrange("(k p) h -> p k h", p=128), s_gw)
    s.dma("gpsimd", rw_sb[:], rw_h.rearrange("(k p) e -> p k e", p=128), s_gw)
    GW_ATTN = s_gw.n
    GW_CONV_E = []
    for e in range(E):
        w1r_in = w1_h[e].rearrange("(kt p) (half fp f) -> fp p half kt f",
                                   p=128, half=2, f=128)
        for fp in range(11):
            for half in range(2):
                s.dma("gpsimd", w1bf_h[e, fp, :, half], w1r_in[fp, :, half], s_gw)
        w2r_in = w2_h[e].rearrange("(ft p) (hh h) -> hh p ft h", p=128, hh=2)
        for hh in range(2):
            s.dma("gpsimd", w2bf_h[e, hh], w2r_in[hh], s_gw)
        GW_CONV_E.append(s_gw.n)

    s.wait("vector", s_ld, LD_CONSTS)
    s.op("vector", lambda v: v.memset(ones_pl[:], 1.0))
    s.op("vector", lambda v: v.memset(carry_r[:], 0.0))
    s.op("vector", lambda v: v.memset(carry_i[:], 0.0), inc=s_dv)
    V_INIT = s_dv.n
    s.wait("scalar", s_ld, LD_CONSTS)
    s.op("scalar", lambda sc: sc.activation(tc0[:], arp_c[:], AF.Abs))
    s.op("scalar", lambda sc: sc.activation(tc3[:], arp_c[:], AF.Relu))
    s.op("scalar", lambda sc: sc.activation(tc1[:], tc0[:], AF.Exp, scale=ccol(C_NEG1)))
    s.op("scalar", lambda sc: sc.copy(bar[:, 4:5], cc[:, 0:1]))
    s.op("scalar", lambda sc: sc.activation(tc0[:], tc1[:], AF.Ln, bias=ccol(C_ONE)))
    aflush(s_ac)
    A_SP = s_ac.n
    s.wait("vector", s_ac, A_SP)
    s.op("vector", lambda v: v.tensor_add(msp_c[:], tc0[:], tc3[:]))
    vflush(s_dv)
    s.op("vector", lambda v: v.tensor_single_scalar(msp_c[:], msp_c[:], -8.0, ALU.mult))
    vflush(s_dv)
    V_MSP = s_dv.n

    pa_free = [("", 0), ("", 0)]
    p1 = dict(V_end=0, L_t_prev=0, P_bdl_prev=0, P_wo_prev=0,
              S_p_prev=0, mm_rot=0, V_h=0, c_front=-1, G_carry=0, V_scans=0)
    Vend_hist = [0] * (NC1 + 2)
    S_pscr = [0] * NC1
    S_xnt = [0] * NC1
    p2 = dict(py_free=[0, 0], po_free=[0, 0], xn2_freeby=[0, 0],
              w1_free=[0] * 2, w2_free=[0] * 2, S_part_prev=0,
              gft=0, ghh=0, gtt=0, V_lp=0)
    W1R_, W2R_ = 2, 2

    x_r = x_h.rearrange("s (c p) ch -> (s c) p ch", p=128)
    y_r = y_h.rearrange("s (k tt p) ch -> (s k) p tt ch", tt=NT2, p=128)

    xr = xh_f.ap().rearrange("p (a ct) t -> p a ct t", a=2)[:, 0]
    xi = xh_f.ap().rearrange("p (a ct) t -> p a ct t", a=2)[:, 1]
    hr = h_bf.rearrange("p (a ct) t -> p a ct t", a=2)[:, 0]
    hi = h_bf.rearrange("p (a ct) t -> p a ct t", a=2)[:, 1]
    mspb = msp_c.ap().broadcast_to([128, 8, L1])
    aipb = aip_c.ap().broadcast_to([128, 8, L1])
    prv = p_o0.ap().rearrange("p (tt e) -> p tt e", e=E)[:, 0:NT2, :]

    def pa_wait(bk):
        nm, v = pa_free[bk]
        if nm == "ac":
            s.wait("tensor", s_ac, v)
        elif nm == "dv":
            s.wait("tensor", s_dv, v)

    def ph1_gen(c):
        ci = c % (NC1 // SEQ)
        x_tm = x_tm2[c % 2]
        if c >= 2:
            s.wait("sync", s_dv, Vend_hist[c - 2])
        s.dma("sync", x_tm[:], x_r[c], s_ld)
        L_x = s_ld.n

        if ci == 0 and c > 0:
            s.op("gpsimd", lambda g: g.memset(carry_r[:], 0.0))
            s.op("gpsimd", lambda g: g.memset(carry_i[:], 0.0))
            gflush(s_gp)

        s.wait("vector", s_ld, L_x)
        s.wait("vector", s_ld, p1["L_t_prev"])
        s.op("vector", lambda v: v.reduce_sum(lnc[:, 6:7], x_tm[:], AX.X), inc=s_dv)
        s.wait("scalar", s_ld, L_x)
        s.wait("scalar", s_gp, p1["G_carry"])
        s.op("scalar", lambda sc: sc.activation(sqv, x_tm[:], AF.Square), inc=s_ac)
        A_sq = s_ac.n
        s.wait("vector", s_ac, A_sq)
        s.op("vector", lambda v: v.reduce_sum(s2c[:], sqv, AX.X))
        s.op("vector", lambda v: v.tensor_single_scalar(lnc[:, 2:3], lnc[:, 6:7],
                                                        1.0 / D, ALU.mult))
        s.op("vector", lambda v: v.tensor_single_scalar(lnc[:, 5:6], s2c[:],
                                                        1.0 / D, ALU.mult))
        s.op("vector", lambda v: v.tensor_mul(lnc[:, 3:4], lnc[:, 2:3], lnc[:, 2:3]))
        vsp()
        s.op("vector", lambda v: v.tensor_sub(lnc[:, 1:2], lnc[:, 5:6], lnc[:, 3:4]))
        vflush(s_dv)
        V_var = s_dv.n
        s.wait("scalar", s_dv, V_var)
        s.op("scalar", lambda sc: sc.activation(lnc[:, 4:5], lnc[:, 1:2], AF.Ln,
                                                bias=ccol(C_EPS)))
        s.op("scalar", lambda sc: sc.copy(bar[:, 4:5], cc[:, 0:1]))
        s.op("scalar", lambda sc: sc.activation(lnc[:, 0:1], lnc[:, 4:5], AF.Exp,
                                                scale=ccol(C_NEGHALF)))
        aflush(s_ac)
        A_rstd = s_ac.n
        s.wait("vector", s_ac, A_rstd)
        s.op("vector", lambda v: v.tensor_scalar(
            xn_tm[:], x_tm[:], lnc[:, 2:3], lnc[:, 0:1], ALU.subtract, ALU.mult),
            inc=s_dv)
        V_ln = s_dv.n

        yield
        buf = (c // 4) % 2
        q = c % 4
        s.wait("sync", s_dv, V_ln)
        if p2["xn2_freeby"][buf]:
            s.wait("sync", s_pe, p2["xn2_freeby"][buf])
        s.dmat("sync", xn2[buf][:, :, q * 128:(q + 1) * 128], xn_tm[:], s_ld)
        L_t = s_ld.n
        p1["L_t_prev"] = L_t
        S_xnt[c] = L_t

        if c == 0:
            s.wait("tensor", s_gw, GW_ATTN)
        s.wait("tensor", s_ld, L_t)
        for ht in range(16):
            bk = p1["mm_rot"] % 2
            p1["mm_rot"] += 1
            pa_wait(bk)
            for kt in range(8):
                s.op("tensor", lambda t, ht=ht, kt=kt, bk=bk, buf=buf, q=q: t.matmul(
                    p_a[bk][:, 0:L1], w_in_sb[:, kt, ht * 128:(ht + 1) * 128],
                    xn2[buf][:, kt, q * 128:(q + 1) * 128],
                    start=(kt == 0), stop=(kt == 7)),
                    inc=(s_pe if kt == 7 else None))
            P_ht = s_pe.n
            s.wait("vector", s_pe, P_ht)
            if ht == 0:
                s.wait("vector", s_gp, p1.get("G_rot", 0))
            s.op("vector", lambda v, ht=ht, bk=bk: v.tensor_scalar(
                xh_f[:, ht, :], p_a[bk][:, 0:L1], bin_c[:, ht:ht + 1], None,
                ALU.add), inc=s_dv)
            V_xh = s_dv.n
            if ht == 0:
                s.wait("scalar", s_pe, p1["P_bdl_prev"])
                s.wait("scalar", s_gp, p1["G_carry"])
            s.wait("scalar", s_dv, V_xh)
            s.op("scalar", lambda sc, ht=ht, bk=bk: sc.activation(
                xh_b[:, ht, :], p_a[bk][:, 0:L1], AF.Identity,
                bias=bin_c[:, ht:ht + 1]), inc=s_ac)
            pa_free[bk] = ("ac", s_ac.n)

        yield
        s.wait("scalar", s_dv, p1["V_scans"])
        s.wait("scalar", s_gp, p1["G_carry"])
        for wsb, bcol, gpl in ((wgx_sb, bgx_c, GX), (wga_sb, bga_c, GA)):
            for b in range(NB):
                bk = p1["mm_rot"] % 2
                p1["mm_rot"] += 1
                pa_wait(bk)
                for kt in range(2):
                    s.op("tensor", lambda t, wsb=wsb, b=b, kt=kt, bk=bk: t.matmul(
                        p_a[bk][:, 0:L1], wsb[:, b, kt, :],
                        xh_b[:, b * 2 + kt, :], start=(kt == 0), stop=(kt == 1)),
                        inc=(s_pe if kt == 1 else None))
                P_g = s_pe.n
                s.wait("scalar", s_pe, P_g)
                s.op("scalar", lambda sc, gpl=gpl, b=b, bk=bk, bcol=bcol: sc.activation(
                    gpl[:, b, :], p_a[bk][:, 0:L1], AF.Sigmoid,
                    bias=bcol[:, b:b + 1]), inc=s_ac)
                pa_free[bk] = ("ac", s_ac.n)
        p1["P_bdl_prev"] = s_pe.n
        A_gates = s_ac.n

        yield
        if c == 0:
            s.wait("vector", s_dv, max(V_MSP, V_INIT))
        s.wait("vector", s_ac, A_gates)
        s.op("vector", lambda v: v.tensor_tensor(T0[:], GA[:], mspb, ALU.mult))
        s.op("vector", lambda v: v.tensor_tensor(T1[:], GA[:], aipb, ALU.mult),
             inc=s_dv)
        s.wait("vector", s_gp, p1["G_carry"])
        for ct in range(8):
            s.op("vector", lambda v, ct=ct: v.tensor_tensor_scan(
                T2[:, ct, :], ones_pl[:], T1[:, ct, :], 0.0, ALU.mult, ALU.add))
        s.op("vector", lambda v: v.tensor_single_scalar(
            T1[:], T2[:], 1.0 / (2 * PI), ALU.mult))
        s.op("vector", lambda v: v.tensor_copy(KIap[:], T1[:]))
        s.op("vector", lambda v: v.tensor_copy(T1[:], KIap[:]))
        s.op("vector", lambda v: v.scalar_tensor_tensor(
            T1[:], T1[:], -2 * PI, T2[:], ALU.mult, ALU.add), inc=s_dv)
        V_phi = s_dv.n
        yield
        s.wait("scalar", s_dv, V_phi)
        s.op("scalar", lambda sc: sc.activation(T2[:], T0[:], AF.Exp))
        s.op("scalar", lambda sc: sc.activation(T0[:], T2[:], AF.Square), inc=s_ac)
        A_q = s_ac.n
        s.wait("vector", s_ac, A_q)
        s.op("vector", lambda v: v.tensor_scalar(
            T0[:], T0[:], -1.0, 1.0, ALU.mult, ALU.add))
        s.op("vector", lambda v: v.tensor_single_scalar(
            T0[:], T0[:], 1e-6, ALU.max), inc=s_dv)
        V_r2 = s_dv.n
        s.wait("scalar", s_dv, V_r2)
        s.op("scalar", lambda sc: sc.activation(T3[:], T0[:], AF.Ln))
        s.op("scalar", lambda sc: sc.activation(T0[:], T3[:], AF.Exp,
                                                scale=ccol(C_HALF)), inc=s_ac)
        s.op("scalar", lambda sc: sc.activation(T3[:], T1[:], AF.Sin))
        s.op("scalar", lambda sc: sc.activation(T5[:], T1[:], AF.Abs))
        s.op("scalar", lambda sc: sc.activation(
            T4[:], T5[:], AF.Sin, scale=ccol(C_NEG1), bias=ccol(C_PIH)), inc=s_ac)
        A_trig = s_ac.n
        yield
        s.wait("gpsimd", s_ac, A_trig)
        s.wait("gpsimd", s_dv, V_phi)
        s.op("gpsimd", lambda g: g.tensor_mul(T1[:], GX[:], T0[:]))
        s.op("gpsimd", lambda g: g.tensor_mul(GX[:], T4[:], T1[:]))
        s.op("gpsimd", lambda g: g.tensor_mul(T0[:], T3[:], T1[:]))
        s.op("gpsimd", lambda g: g.tensor_mul(T1[:], GX[:], xr))
        s.op("gpsimd", lambda g: g.tensor_mul(T5[:], T0[:], xi))
        s.op("gpsimd", lambda g: g.tensor_add(T1[:], T1[:], T5[:]))
        s.op("gpsimd", lambda g: g.tensor_mul(GA[:], GX[:], xi))
        s.op("gpsimd", lambda g: g.tensor_mul(T5[:], T0[:], xr))
        s.op("gpsimd", lambda g: g.tensor_sub(GA[:], GA[:], T5[:]))
        gflush(s_gp)
        G_rot = s_gp.n
        p1["G_rot"] = G_rot
        yield
        s.wait("vector", s_gp, G_rot)
        for ct in range(8):
            s.op("vector", lambda v, ct=ct: v.tensor_tensor_scan(
                GX[:, ct, :], T2[:, ct, :], T1[:, ct, :], carry_r[:, ct:ct + 1],
                ALU.mult, ALU.add))
        for ct in range(8):
            s.op("vector", lambda v, ct=ct: v.tensor_tensor_scan(
                T0[:, ct, :], T2[:, ct, :], GA[:, ct, :], carry_i[:, ct:ct + 1],
                ALU.mult, ALU.add))
        vflush(s_dv)
        V_scans = s_dv.n
        p1["V_scans"] = V_scans
        s.wait("gpsimd", s_dv, V_scans)
        s.wait("gpsimd", s_pe, p1["P_wo_prev"])
        s.op("gpsimd", lambda g: g.tensor_mul(T2[:], T4[:], GX[:]))
        s.op("gpsimd", lambda g: g.tensor_mul(T5[:], T3[:], T0[:]))
        s.op("gpsimd", lambda g: g.tensor_sub(hr, T2[:], T5[:]))
        s.op("gpsimd", lambda g: g.tensor_mul(T2[:], T3[:], GX[:]))
        s.op("gpsimd", lambda g: g.tensor_mul(T5[:], T4[:], T0[:]))
        s.op("gpsimd", lambda g: g.tensor_add(hi, T2[:], T5[:]))
        LC = L1 - 1
        s.op("gpsimd", lambda g: g.tensor_mul(tc0[:], T4[:, :, LC], GX[:, :, LC]))
        s.op("gpsimd", lambda g: g.tensor_mul(tc1[:], T3[:, :, LC], T0[:, :, LC]))
        s.op("gpsimd", lambda g: g.tensor_mul(tc2[:], T3[:, :, LC], GX[:, :, LC]))
        s.op("gpsimd", lambda g: g.tensor_mul(tc3[:], T4[:, :, LC], T0[:, :, LC]))
        s.op("gpsimd", lambda g: g.tensor_sub(carry_r[:], tc0[:], tc1[:]))
        s.op("gpsimd", lambda g: g.tensor_add(carry_i[:], tc2[:], tc3[:]))
        gflush(s_gp)
        p1["G_carry"] = s_gp.n
        V_h = s_gp.n
        yield
        yield
        s.wait("tensor", s_gp, V_h)
        pa_wait(0)
        pa_wait(1)
        for kt in range(16):
            for nh in range(2):
                s.op("tensor", lambda t, nh=nh, kt=kt: t.matmul(
                    p_a[nh][:, 0:512], h_bf[:, kt, :],
                    w_out_sb[:, kt, nh * 512:(nh + 1) * 512],
                    start=(kt == 0), stop=(kt == 15)),
                    inc=(s_pe if (nh == 1 and kt == 15) else None))
        P_wo = s_pe.n
        p1["P_wo_prev"] = P_wo
        s.wait("vector", s_pe, P_wo)
        if p1["S_p_prev"]:
            s.wait("vector", s_st, p1["S_p_prev"])
        for nh in range(2):
            sl = slice(nh * 512, (nh + 1) * 512)
            s.op("vector", lambda v, nh=nh, sl=sl: v.scalar_tensor_tensor(
                partial[:, sl], p_a[nh][:, 0:512], 1.0, x_tm[:, sl],
                ALU.mult, ALU.add))
            s.op("vector", lambda v, sl=sl: v.tensor_add(
                partial[:, sl], partial[:, sl], bout_pl[:, sl]), inc=s_dv)
        pa_free[0] = ("dv", s_dv.n)
        pa_free[1] = ("dv", s_dv.n)
        p1["V_end"] = s_dv.n
        Vend_hist[c] = s_dv.n

        s.wait("sync", s_dv, p1["V_end"])
        s.dma("sync", pscr_h[c], partial[:], s_st)
        p1["S_p_prev"] = s_st.n
        S_pscr[c] = s_st.n

    def ph2_gen(c2):
        buf = c2 % 2
        if p2["S_part_prev"]:
            s.wait("sync", s_st, p2["S_part_prev"])
        s.wait("sync", s_st, S_pscr[4 * c2 + 3])
        for q in range(4):
            s.dma("sync", part2[:, q, :], pscr_h[4 * c2 + q], s_p2l)
        p2["P2L"] = s_p2l.n

        s.wait("tensor", s_ld, S_xnt[4 * c2 + 3])
        for po in range(2):
            if p2["po_free"][po]:
                s.wait("tensor", s_dv, p2["po_free"][po])
        for tt in range(NT2):
            for kt in range(8):
                s.op("tensor", lambda t, tt=tt, kt=kt, buf=buf: t.matmul(
                    p_o0[:, tt * E:(tt + 1) * E],
                    xn2[buf][:, kt, tt * 128:(tt + 1) * 128],
                    rw_sb[:, kt, :], start=(kt == 0), stop=(kt == 7)),
                    inc=(s_pe if (tt == NT2 - 1 and kt == 7) else None))
        P_r = s_pe.n
        s.wait("vector", s_pe, P_r)
        s.op("vector", lambda v: v.tensor_reduce(
            sm0[:], prv, AX.X, ALU.max, negate=True))
        vflush(s_dv)
        V_nm = s_dv.n
        s.wait("scalar", s_dv, V_nm)
        for tt in range(NT2):
            s.op("scalar", lambda sc, tt=tt: sc.activation(
                pr4[:, tt, :], prv[:, tt, :], AF.Exp, bias=sm0[:, tt:tt + 1]))
        aflush(s_ac)
        A_exp = s_ac.n
        s.wait("vector", s_ac, A_exp)
        s.op("vector", lambda v: v.reduce_sum(sm1[:], pr4[:], AX.X), inc=s_dv)
        p2["po_free"][0] = s_dv.n
        vsp()
        s.op("vector", lambda v: v.reciprocal(sm0[:], sm1[:]))
        vsp()
        s.op("vector", lambda v: v.tensor_tensor(
            pr4[:], pr4[:], sm0.ap().broadcast_to([128, NT2, E]), ALU.mult))
        vsp()
        s.op("vector", lambda v: v.tensor_reduce(sm1[:], pr4[:], AX.X, ALU.max))
        vsp()
        s.op("vector", lambda v: v.tensor_tensor(
            lp[:], pr4[:], sm1.ap().broadcast_to([128, NT2, E]), ALU.is_lt))
        vsp()
        s.op("vector", lambda v: v.tensor_mul(lp[:], lp[:], pr4[:]))
        vsp()
        s.op("vector", lambda v: v.tensor_reduce(sm1[:], lp[:], AX.X, ALU.max))
        vsp()
        s.op("vector", lambda v: v.tensor_tensor(
            lp[:], pr4[:], sm1.ap().broadcast_to([128, NT2, E]), ALU.is_ge))
        vsp()
        s.op("vector", lambda v: v.tensor_mul(lp[:], lp[:], pr4[:]))
        vflush(s_dv)
        p2["V_lp"] = s_dv.n
        yield
        if debug and c2 == dbgc2:
            s.wait("sync", s_dv, p2["V_lp"])
            s.dma("sync", dbg_lp, lp[:], s_ld)
            s.dma("sync", dbg_pr4, pr4[:], s_ld)
            s.dma("sync", dbg_sm, sm0[:], s_ld)
            s.dma("sync", dbg_xn2, xn2[c2 % 2][:], s_ld)
            s.dma("sync", dbg_part2, part2[:], s_ld)
            s.wait("vector", s_ld)
            s.wait("scalar", s_ld)
        for e in range(E):
            for _u in ph2_expert(c2, e):
                yield
        V_acc = s_dv.n
        s.wait("sync", s_dv, V_acc)
        s.dma("sync", y_r[c2], part2[:], s_st)
        p2["S_part_prev"] = s_st.n

    def ph2_expert(c2, e):
        buf = c2 % 2
        if c2 == 0:
            s.wait("sync", s_gw, GW_CONV_E[e])
        G_hh = []
        for hh in range(2):
            wslot = p2["ghh"] % W2R_
            p2["ghh"] += 1
            if p2["w2_free"][wslot]:
                s.wait("sync", s_pe, p2["w2_free"][wslot])
            s.dma("sync", w2r[:, wslot], w2bf_h[e, hh], s_w2s[wslot])
            G_hh.append((wslot, s_w2s[wslot].n))

        if e == 0:
            s.wait("tensor", s_dv, p2["V_lp"])
        for ft in range(11):
            slot = p2["gft"] % W1R_
            p2["gft"] += 1
            if p2["w1_free"][slot]:
                s.wait("sync", s_pe, p2["w1_free"][slot])
            s.dma("sync", w1r[:, slot], w1bf_h[e, ft], s_w1s[slot])
            gval = s_w1s[slot].n
            par = ft % 2
            s.wait("tensor", s_w1s[slot], gval)
            if p2["py_free"][par]:
                s.wait("tensor", s_dv, p2["py_free"][par])
            for half in range(2):
                for kt in range(8):
                    s.op("tensor", lambda t, slot=slot, half=half, kt=kt, par=par,
                         buf=buf: t.matmul(
                        p_y[par][half][:],
                        w1r[:, slot, half, kt, :],
                        xn2[buf][:, kt, :],
                        start=(kt == 0), stop=(kt == 7)),
                        inc=(s_pe if (half == 1 and kt == 7) else None))
            P_yft = s_pe.n
            p2["w1_free"][slot] = P_yft
            s.wait("scalar", s_pe, P_yft)
            if p2["py_free"][par ^ 1]:
                s.wait("scalar", s_dv, p2["py_free"][par ^ 1])
            s.op("scalar", lambda sc, par=par: sc.activation(
                silu_sb[par][:], p_y[par][0][:], AF.Silu), inc=s_ac)
            A_sil = s_ac.n
            s.wait("vector", s_ac, A_sil)
            s.op("vector", lambda v, ft=ft, par=par: v.tensor_mul(
                gbuf[:, ft, :], silu_sb[par][:], p_y[par][1][:]), inc=s_dv)
            p2["py_free"][par] = s_dv.n
            yield
        V_g = s_dv.n
        if debug and c2 == dbgc2 and e == 0:
            s.wait("sync", s_dv, V_g)
            s.dma("sync", dbg_g, gbuf[:], s_ld)
            s.wait("tensor", s_ld)
        if e == E - 1:
            p2["xn2_freeby"][buf] = s_pe.n

        s.wait("tensor", s_dv, V_g)
        if e == 0:
            s.wait("vector", s_p2l, p2["P2L"])
        for hh in range(2):
            wslot, gval = G_hh[hh]
            s.wait("tensor", s_w2s[wslot], gval)
            for tt in range(NT2):
                po = p2["gtt"] % 2
                p2["gtt"] += 1
                if p2["po_free"][po]:
                    s.wait("tensor", s_dv, p2["po_free"][po])
                for ft in range(11):
                    s.op("tensor", lambda t, tt=tt, ft=ft, po=po, wslot=wslot: t.matmul(
                        p_o[po][:], gbuf[:, ft, tt * 128:(tt + 1) * 128],
                        w2r[:, wslot, ft, :],
                        start=(ft == 0), stop=(ft == 10)),
                        inc=(s_pe if ft == 10 else None))
                P_oms = s_pe.n
                s.wait("vector", s_pe, P_oms)
                sl = slice(hh * 512, (hh + 1) * 512)
                s.op("vector", lambda v, tt=tt, po=po, e=e, sl=sl: v.scalar_tensor_tensor(
                    part2[:, tt, sl], p_o[po][:], lp[:, tt, e:e + 1],
                    part2[:, tt, sl], ALU.mult, ALU.add), inc=s_dv)
                p2["po_free"][po] = s_dv.n
                yield
            p2["w2_free"][wslot] = s_pe.n
        if debug and c2 == dbgc2 and e == 0:
            s.wait("sync", s_dv, s_dv.n)
            s.dma("sync", dbg_p2, part2[:], s_ld)
            s.wait("tensor", s_ld)
            s.wait("vector", s_ld)

    ph2q = []
    next_c2 = 0

    def pump_ph2(n):
        emitted = 0
        while emitted < n and ph2q:
            try:
                next(ph2q[0][1])
                emitted += 1
            except StopIteration:
                ph2q.pop(0)

    for k in range(NC1):
        while ph2q and ph2q[0][0] <= k // 4 - 2:
            try:
                next(ph2q[0][1])
            except StopIteration:
                ph2q.pop(0)
        for _u in ph1_gen(k):
            pump_ph2(2)
        if next_c2 < NC2 and k >= 4 * next_c2 + PH2_LAG - 1:
            ph2q.append((next_c2, ph2_gen(next_c2)))
            next_c2 += 1
        pump_ph2(2)
    while next_c2 < NC2:
        ph2q.append((next_c2, ph2_gen(next_c2)))
        next_c2 += 1
    while ph2q:
        try:
            next(ph2q[0][1])
        except StopIteration:
            ph2q.pop(0)

    s.wait("sync", s_st, s_st.n)

    s.emit(nc, "mk")
    return nc


def make_in_maps(L0_attn_x, L0_attn_w_in, L0_attn_b_in, L0_attn_w_gx, L0_attn_b_gx,
                 L0_attn_w_ga, L0_attn_b_ga, L0_attn_a_real_param, L0_attn_a_imag_param,
                 L0_attn_w_out, L0_attn_b_out, L0_ffn_router_weight, L0_ffn_w1,
                 L0_ffn_w2):
    f32 = np.float32
    cc = np.zeros((128, 8), f32)
    cc[:, C_ONE] = 1.0
    cc[:, C_NEG1] = -1.0
    cc[:, C_HALF] = 0.5
    cc[:, C_NEGHALF] = -0.5
    cc[:, C_PIH] = PI / 2
    cc[:, C_EPS] = 1e-5
    cc[:, C_NEG2PI] = -2 * PI
    cc[:, C_I2PI] = 1.0 / (2 * PI)
    shared = dict(
        w_in=np.ascontiguousarray(L0_attn_w_in, f32),
        b_in=np.ascontiguousarray(np.asarray(L0_attn_b_in, f32).reshape(16, 128).T),
        w_gx=np.ascontiguousarray(L0_attn_w_gx, f32),
        b_gx=np.ascontiguousarray(np.asarray(L0_attn_b_gx, f32).reshape(8, 128).T),
        w_ga=np.ascontiguousarray(L0_attn_w_ga, f32),
        b_ga=np.ascontiguousarray(np.asarray(L0_attn_b_ga, f32).reshape(8, 128).T),
        arp=np.ascontiguousarray(np.asarray(L0_attn_a_real_param, f32).reshape(8, 128).T),
        aip=np.ascontiguousarray(np.asarray(L0_attn_a_imag_param, f32).reshape(8, 128).T),
        w_out=np.ascontiguousarray(L0_attn_w_out, f32),
        bout_pl=np.ascontiguousarray(
            np.broadcast_to(np.asarray(L0_attn_b_out, f32), (128, D))),
        rw=np.ascontiguousarray(L0_ffn_router_weight, f32),
        w1=np.ascontiguousarray(L0_ffn_w1, f32),
        w2=np.ascontiguousarray(L0_ffn_w2, f32),
        cc=cc,
    )
    x = np.ascontiguousarray(np.asarray(L0_attn_x, f32))
    in_maps = []
    for i in range(8):
        m = dict(shared)
        m["x"] = np.ascontiguousarray(x[2 * i:2 * i + 2])
        in_maps.append(m)
    return in_maps


def kernel(**inputs):
    in_maps = make_in_maps(**inputs)
    nc = build_nc()
    res = run_bass_kernel_spmd(nc, in_maps, core_ids=list(range(8)))
    return np.concatenate([np.asarray(r["y"], np.float32) for r in res.results], axis=0)


# version 10
# speedup vs baseline: 1.1935x; 1.0765x over previous
import numpy as np
import ml_dtypes
import concourse.bass as bass
import concourse.mybir as mybir
import concourse.bass_utils as _bass_utils
from concourse.bass_utils import run_bass_kernel_spmd

if not getattr(_bass_utils, "_ldw_opt_patched", False):
    _orig_run_command = _bass_utils.run_command

    def _run_command_ldw(cmd, *a, **kw):
        if isinstance(cmd, list):
            cmd = ["--enable-ldw-opt=true" if c == "--enable-ldw-opt=false" else c
                   for c in cmd]
        return _orig_run_command(cmd, *a, **kw)

    _bass_utils.run_command = _run_command_ldw
    _bass_utils._ldw_opt_patched = True

F32 = mybir.dt.float32
BF16 = mybir.dt.bfloat16
I32 = mybir.dt.int32
AF = mybir.ActivationFunctionType
ALU = mybir.AluOpType
AX = mybir.AxisListType

PI = float(np.pi)

SEQ = 2
T = 4096
D = 1024
DH = 2048
NB = 8
E = 8
FF = 1408
L1 = 128
NC1 = SEQ * T // L1
L2 = 512
NT2 = L2 // 128
NC2 = SEQ * T // L2
PH2_LAG = 5

C_ONE, C_NEG1, C_HALF, C_NEGHALF, C_PIH, C_EPS, C_NEG2PI, C_I2PI = range(8)

ENGS = ("sync", "gpsimd", "tensor", "vector", "scalar")


class Ctr:
    def __init__(self, nc, name):
        self.sem = nc.alloc_semaphore(name)
        self.n = 0


class Sched:
    def __init__(self):
        self.q = {e: [] for e in ENGS}

    def on(self, eng, fn):
        self.q[eng].append(fn)

    def wait(self, eng, ctr, val=None):
        v = ctr.n if val is None else val
        if v <= 0:
            return
        sem = ctr.sem
        self.on(eng, lambda e, s=sem, vv=v: e.wait_ge(s, vv))

    def op(self, eng, fn, inc=None, k=1):
        if inc is not None:
            inc.n += k
            sem = inc.sem
            self.on(eng, lambda e, f=fn, s=sem, kk=k: f(e).then_inc(s, kk))
        else:
            self.on(eng, fn)

    def dma(self, eng, out, in_, inc):
        inc.n += 16
        sem = inc.sem
        self.on(eng, lambda e, o=out, i=in_, s=sem: e.dma_start(out=o, in_=i).then_inc(s, 16))

    def dmat(self, eng, out, in_, inc):
        inc.n += 16
        sem = inc.sem
        self.on(eng, lambda e, o=out, i=in_, s=sem: e.dma_start_transpose(o, i).then_inc(s, 16))

    def emit(self, nc, name):
        with nc.Block(name) as block:
            @block.sync
            def _(e):
                for fn in self.q["sync"]:
                    fn(e)

            @block.gpsimd
            def _(e):
                for fn in self.q["gpsimd"]:
                    fn(e)

            @block.tensor
            def _(e):
                for fn in self.q["tensor"]:
                    fn(e)

            @block.vector
            def _(e):
                for fn in self.q["vector"]:
                    fn(e)

            @block.scalar
            def _(e):
                for fn in self.q["scalar"]:
                    fn(e)


def build_nc(debug=False, dbgc2=0):
    nc = bass.Bass("TRN2", target_bir_lowering=False)

    x_h = nc.dram_tensor("x", [SEQ, T, D], F32, kind="ExternalInput").ap()
    w_in_h = nc.dram_tensor("w_in", [D, DH], F32, kind="ExternalInput").ap()
    b_in_h = nc.dram_tensor("b_in", [128, 16], F32, kind="ExternalInput").ap()
    w_gx_h = nc.dram_tensor("w_gx", [NB, 256, 128], F32, kind="ExternalInput").ap()
    b_gx_h = nc.dram_tensor("b_gx", [128, 8], F32, kind="ExternalInput").ap()
    w_ga_h = nc.dram_tensor("w_ga", [NB, 256, 128], F32, kind="ExternalInput").ap()
    b_ga_h = nc.dram_tensor("b_ga", [128, 8], F32, kind="ExternalInput").ap()
    arp_h = nc.dram_tensor("arp", [128, 8], F32, kind="ExternalInput").ap()
    aip_h = nc.dram_tensor("aip", [128, 8], F32, kind="ExternalInput").ap()
    w_out_h = nc.dram_tensor("w_out", [DH, D], F32, kind="ExternalInput").ap()
    bout_pl_h = nc.dram_tensor("bout_pl", [128, D], F32, kind="ExternalInput").ap()
    rw_h = nc.dram_tensor("rw", [D, E], F32, kind="ExternalInput").ap()
    w1_h = nc.dram_tensor("w1", [E, D, 2 * FF], F32, kind="ExternalInput").ap()
    w2_h = nc.dram_tensor("w2", [E, FF, D], F32, kind="ExternalInput").ap()
    cc_h = nc.dram_tensor("cc", [128, 8], F32, kind="ExternalInput").ap()

    y_h = nc.dram_tensor("y", [SEQ, T, D], F32, kind="ExternalOutput").ap()

    _scr = "ExternalOutput" if debug else "Internal"
    w1bf_h = nc.dram_tensor("w1bf", [E, 11, 128, 2, 8, 128], BF16, kind=_scr).ap()
    w2bf_h = nc.dram_tensor("w2bf", [E, 2, 128, 11, 512], BF16, kind=_scr).ap()
    pscr_h = nc.dram_tensor("pscr", [NC1, 128, D], F32, kind=_scr).ap()
    if debug:
        dbg_lp = nc.dram_tensor("dbg_lp", [128, NT2, E], F32, kind="ExternalOutput").ap()
        dbg_pr4 = nc.dram_tensor("dbg_pr4", [128, NT2, E], F32, kind="ExternalOutput").ap()
        dbg_sm = nc.dram_tensor("dbg_sm", [128, NT2], F32, kind="ExternalOutput").ap()
        dbg_xn2 = nc.dram_tensor("dbg_xn2", [128, 8, L2], BF16, kind="ExternalOutput").ap()
        dbg_part2 = nc.dram_tensor("dbg_part2", [128, NT2, D], F32, kind="ExternalOutput").ap()
        dbg_g = nc.dram_tensor("dbg_g", [128, 11, L2], BF16, kind="ExternalOutput").ap()
        dbg_p2 = nc.dram_tensor("dbg_p2", [128, NT2, D], F32, kind="ExternalOutput").ap()

    cc = nc.alloc_sbuf_tensor("cc_sb", [128, 8], F32)
    bout_pl = nc.alloc_sbuf_tensor("bout_sb", [128, D], F32)
    rw_sb = nc.alloc_sbuf_tensor("rw_sb", [128, 8, E], BF16)
    bin_c = nc.alloc_sbuf_tensor("bin_c", [128, 16], F32)
    bgx_c = nc.alloc_sbuf_tensor("bgx_c", [128, 8], F32)
    bga_c = nc.alloc_sbuf_tensor("bga_c", [128, 8], F32)
    arp_c = nc.alloc_sbuf_tensor("arp_c", [128, 8], F32)
    aip_c = nc.alloc_sbuf_tensor("aip_c", [128, 8], F32)
    msp_c = nc.alloc_sbuf_tensor("msp_c", [128, 8], F32)
    tc0 = nc.alloc_sbuf_tensor("tc0", [128, 8], F32)
    tc1 = nc.alloc_sbuf_tensor("tc1", [128, 8], F32)
    tc2 = nc.alloc_sbuf_tensor("tc2", [128, 8], F32)
    tc3 = nc.alloc_sbuf_tensor("tc3", [128, 8], F32)
    bar = nc.alloc_sbuf_tensor("bar", [128, 8], F32)
    ones_pl = nc.alloc_sbuf_tensor("ones_pl", [128, L1], F32)
    lnc = nc.alloc_sbuf_tensor("lnc", [128, 8], F32)
    s2c = nc.alloc_sbuf_tensor("s2c", [128, 1], F32)
    carry_r = nc.alloc_sbuf_tensor("carry_r", [128, 8], F32)
    carry_i = nc.alloc_sbuf_tensor("carry_i", [128, 8], F32)

    w_in_sb = nc.alloc_sbuf_tensor("w_in_sb", [128, 8, DH], BF16)
    wgx_sb = nc.alloc_sbuf_tensor("wgx_sb", [128, NB, 2, 128], BF16)
    wga_sb = nc.alloc_sbuf_tensor("wga_sb", [128, NB, 2, 128], BF16)
    w_out_sb = nc.alloc_sbuf_tensor("w_out_sb", [128, 16, D], BF16)
    x_tm2 = [nc.alloc_sbuf_tensor(f"x_tm{i}", [128, D], F32) for i in range(2)]
    xn_tm = nc.alloc_sbuf_tensor("xn_tm", [128, D], BF16)
    xh_f = nc.alloc_sbuf_tensor("xh_f", [128, 16, L1], F32)
    GA = nc.alloc_sbuf_tensor("GA", [128, 8, L1], F32)
    GX = nc.alloc_sbuf_tensor("GX", [128, 8, L1], F32)
    T0 = nc.alloc_sbuf_tensor("T0", [128, 8, L1], F32)
    T1 = nc.alloc_sbuf_tensor("T1", [128, 8, L1], F32)
    T2 = nc.alloc_sbuf_tensor("T2", [128, 8, L1], F32)
    T3 = nc.alloc_sbuf_tensor("T3", [128, 8, L1], F32)
    T4 = nc.alloc_sbuf_tensor("T4", [128, 8, L1], F32)
    T5 = nc.alloc_sbuf_tensor("T5", [128, 8, L1], F32)
    partial = nc.alloc_sbuf_tensor("partial", [128, D], F32)

    KIap = T5.ap().bitcast(I32)
    h_bf = T1.ap().bitcast(BF16).rearrange("p c (x t) -> p (c x) t", x=2)
    xh_b = T3.ap().bitcast(BF16).rearrange("p c (x t) -> p (c x) t", x=2)
    sqv = T0.ap().rearrange("p a b -> p (a b)")

    W1R = 2
    w1r = nc.alloc_sbuf_tensor("w1r", [128, W1R, 2, 8, 128], BF16)
    W2R = 2
    w2r = nc.alloc_sbuf_tensor("w2r", [128, W2R, 11, 512], BF16)
    gbuf = nc.alloc_sbuf_tensor("gbuf", [128, 11, L2], BF16)
    silu_sb = [nc.alloc_sbuf_tensor("silu_0", [128, L2], F32)] * 2
    xn2 = [nc.alloc_sbuf_tensor(f"xn2_{i}", [128, 8, L2], BF16) for i in range(2)]
    part2 = nc.alloc_sbuf_tensor("part2", [128, NT2, D], F32)
    lp = nc.alloc_sbuf_tensor("lp", [128, NT2, E], F32)
    pr4 = nc.alloc_sbuf_tensor("pr4", [128, NT2, E], F32)
    sm0 = nc.alloc_sbuf_tensor("sm0", [128, NT2], F32)
    sm1 = nc.alloc_sbuf_tensor("sm1", [128, NT2], F32)

    p_a0 = nc.alloc_psum_tensor("p_a0", [128, 512], F32)
    p_a1 = nc.alloc_psum_tensor("p_a1", [128, 512], F32)
    p_y00 = nc.alloc_psum_tensor("p_y00", [128, 512], F32)
    p_y01 = nc.alloc_psum_tensor("p_y01", [128, 512], F32)
    p_y10 = nc.alloc_psum_tensor("p_y10", [128, 512], F32)
    p_y11 = nc.alloc_psum_tensor("p_y11", [128, 512], F32)
    p_o0 = nc.alloc_psum_tensor("p_o0", [128, 512], F32)
    p_o1 = nc.alloc_psum_tensor("p_o1", [128, 512], F32)
    p_a = [p_a0, p_a1]
    p_y = [[p_y00, p_y01], [p_y10, p_y11]]
    p_o = [p_o0, p_o1]

    s_ld = Ctr(nc, "s_ld")
    s_gw = Ctr(nc, "s_gw")
    s_pe = Ctr(nc, "s_pe")
    s_dv = Ctr(nc, "s_dv")
    s_ac = Ctr(nc, "s_ac")
    s_st = Ctr(nc, "s_st")
    s_w1s = [Ctr(nc, "s_w1s0"), Ctr(nc, "s_w1s1")]
    s_w2s = [Ctr(nc, "s_w2s0"), Ctr(nc, "s_w2s1")]
    s_p2l = Ctr(nc, "s_p2l")
    s_gp = Ctr(nc, "s_gp")

    def ccol(i):
        return cc[:, i:i + 1]

    s = Sched()

    def vflush(inc):
        s.op("vector", lambda v: v.tensor_copy(bar[:, 2:3], cc[:, 0:1]), inc=inc)

    def aflush(inc):
        s.op("scalar", lambda sc: sc.copy(bar[:, 3:4], cc[:, 0:1]), inc=inc)

    def vsp():
        s.op("vector", lambda v: v.tensor_copy(bar[:, 5:6], cc[:, 1:2]))

    def gflush(inc):
        s.op("gpsimd", lambda g: g.tensor_copy(bar[:, 6:7], cc[:, 0:1]), inc=inc)

    s.dma("sync", cc[:], cc_h, s_ld)
    s.dma("sync", bout_pl[:], bout_pl_h, s_ld)
    s.dma("sync", bin_c[:], b_in_h, s_ld)
    s.dma("sync", bgx_c[:], b_gx_h, s_ld)
    s.dma("sync", bga_c[:], b_ga_h, s_ld)
    s.dma("sync", arp_c[:], arp_h, s_ld)
    s.dma("sync", aip_c[:], aip_h, s_ld)
    LD_CONSTS = s_ld.n

    s.dma("gpsimd", w_in_sb[:], w_in_h.rearrange("(k p) h -> p k h", p=128), s_gw)
    s.dma("gpsimd", wgx_sb[:], w_gx_h.rearrange("b (kt p) o -> p b kt o", p=128), s_gw)
    s.dma("gpsimd", wga_sb[:], w_ga_h.rearrange("b (kt p) o -> p b kt o", p=128), s_gw)
    s.dma("gpsimd", w_out_sb[:], w_out_h.rearrange("(k p) h -> p k h", p=128), s_gw)
    s.dma("gpsimd", rw_sb[:], rw_h.rearrange("(k p) e -> p k e", p=128), s_gw)
    GW_ATTN = s_gw.n
    GW_CONV_E = []
    for e in range(E):
        w1r_in = w1_h[e].rearrange("(kt p) (half fp f) -> fp p half kt f",
                                   p=128, half=2, f=128)
        for fp in range(11):
            for half in range(2):
                s.dma("gpsimd", w1bf_h[e, fp, :, half], w1r_in[fp, :, half], s_gw)
        w2r_in = w2_h[e].rearrange("(ft p) (hh h) -> hh p ft h", p=128, hh=2)
        for hh in range(2):
            s.dma("gpsimd", w2bf_h[e, hh], w2r_in[hh], s_gw)
        GW_CONV_E.append(s_gw.n)

    s.wait("vector", s_ld, LD_CONSTS)
    s.op("vector", lambda v: v.memset(ones_pl[:], 1.0))
    s.op("vector", lambda v: v.memset(carry_r[:], 0.0))
    s.op("vector", lambda v: v.memset(carry_i[:], 0.0), inc=s_dv)
    V_INIT = s_dv.n
    s.wait("scalar", s_ld, LD_CONSTS)
    s.op("scalar", lambda sc: sc.activation(tc0[:], arp_c[:], AF.Abs))
    s.op("scalar", lambda sc: sc.activation(tc3[:], arp_c[:], AF.Relu))
    s.op("scalar", lambda sc: sc.activation(tc1[:], tc0[:], AF.Exp, scale=ccol(C_NEG1)))
    s.op("scalar", lambda sc: sc.copy(bar[:, 4:5], cc[:, 0:1]))
    s.op("scalar", lambda sc: sc.activation(tc0[:], tc1[:], AF.Ln, bias=ccol(C_ONE)))
    aflush(s_ac)
    A_SP = s_ac.n
    s.wait("vector", s_ac, A_SP)
    s.op("vector", lambda v: v.tensor_add(msp_c[:], tc0[:], tc3[:]))
    vflush(s_dv)
    s.op("vector", lambda v: v.tensor_single_scalar(msp_c[:], msp_c[:], -8.0, ALU.mult))
    vflush(s_dv)
    V_MSP = s_dv.n

    pa_free = [("", 0), ("", 0)]
    p1 = dict(V_end=0, L_t_prev=0, P_bdl_prev=0, P_wo_prev=0,
              S_p_prev=0, mm_rot=0, V_h=0, c_front=-1, G_carry=0, V_scans=0)
    Vend_hist = [0] * (NC1 + 2)
    S_pscr = [0] * NC1
    S_xnt = [0] * NC1
    p2 = dict(py_free=[0, 0], po_free=[0, 0], xn2_freeby=[0, 0],
              w1_free=[0] * 2, w2_free=[0] * 2, S_part_prev=0,
              gft=0, ghh=0, gtt=0, V_lp=0)
    W1R_, W2R_ = 2, 2

    x_r = x_h.rearrange("s (c p) ch -> (s c) p ch", p=128)
    y_r = y_h.rearrange("s (k tt p) ch -> (s k) p tt ch", tt=NT2, p=128)

    xr = xh_f.ap().rearrange("p (a ct) t -> p a ct t", a=2)[:, 0]
    xi = xh_f.ap().rearrange("p (a ct) t -> p a ct t", a=2)[:, 1]
    hr = h_bf.rearrange("p (a ct) t -> p a ct t", a=2)[:, 0]
    hi = h_bf.rearrange("p (a ct) t -> p a ct t", a=2)[:, 1]
    mspb = msp_c.ap().broadcast_to([128, 8, L1])
    aipb = aip_c.ap().broadcast_to([128, 8, L1])
    prv = p_o0.ap().rearrange("p (tt e) -> p tt e", e=E)[:, 0:NT2, :]

    def pa_wait(bk):
        nm, v = pa_free[bk]
        if nm == "ac":
            s.wait("tensor", s_ac, v)
        elif nm == "dv":
            s.wait("tensor", s_dv, v)

    def ph1_gen(c):
        ci = c % (NC1 // SEQ)
        x_tm = x_tm2[c % 2]
        if c >= 2:
            s.wait("sync", s_dv, Vend_hist[c - 2])
        s.dma("sync", x_tm[:], x_r[c], s_ld)
        L_x = s_ld.n

        if ci == 0 and c > 0:
            s.op("gpsimd", lambda g: g.memset(carry_r[:], 0.0))
            s.op("gpsimd", lambda g: g.memset(carry_i[:], 0.0))
            gflush(s_gp)

        s.wait("vector", s_ld, L_x)
        s.wait("vector", s_ld, p1["L_t_prev"])
        s.op("vector", lambda v: v.reduce_sum(lnc[:, 6:7], x_tm[:], AX.X), inc=s_dv)
        s.wait("scalar", s_ld, L_x)
        s.wait("scalar", s_gp, p1["G_carry"])
        s.op("scalar", lambda sc: sc.activation(sqv, x_tm[:], AF.Square), inc=s_ac)
        A_sq = s_ac.n
        s.wait("vector", s_ac, A_sq)
        s.op("vector", lambda v: v.reduce_sum(s2c[:], sqv, AX.X))
        s.op("vector", lambda v: v.tensor_single_scalar(lnc[:, 2:3], lnc[:, 6:7],
                                                        1.0 / D, ALU.mult))
        s.op("vector", lambda v: v.tensor_single_scalar(lnc[:, 5:6], s2c[:],
                                                        1.0 / D, ALU.mult))
        s.op("vector", lambda v: v.tensor_mul(lnc[:, 3:4], lnc[:, 2:3], lnc[:, 2:3]))
        vsp()
        s.op("vector", lambda v: v.tensor_sub(lnc[:, 1:2], lnc[:, 5:6], lnc[:, 3:4]))
        vflush(s_dv)
        V_var = s_dv.n
        s.wait("scalar", s_dv, V_var)
        s.op("scalar", lambda sc: sc.activation(lnc[:, 4:5], lnc[:, 1:2], AF.Ln,
                                                bias=ccol(C_EPS)))
        s.op("scalar", lambda sc: sc.copy(bar[:, 4:5], cc[:, 0:1]))
        s.op("scalar", lambda sc: sc.activation(lnc[:, 0:1], lnc[:, 4:5], AF.Exp,
                                                scale=ccol(C_NEGHALF)))
        aflush(s_ac)
        A_rstd = s_ac.n
        s.wait("vector", s_ac, A_rstd)
        s.op("vector", lambda v: v.tensor_scalar(
            xn_tm[:], x_tm[:], lnc[:, 2:3], lnc[:, 0:1], ALU.subtract, ALU.mult),
            inc=s_dv)
        V_ln = s_dv.n

        yield
        buf = (c // 4) % 2
        q = c % 4
        s.wait("sync", s_dv, V_ln)
        if p2["xn2_freeby"][buf]:
            s.wait("sync", s_pe, p2["xn2_freeby"][buf])
        s.dmat("sync", xn2[buf][:, :, q * 128:(q + 1) * 128], xn_tm[:], s_ld)
        L_t = s_ld.n
        p1["L_t_prev"] = L_t
        S_xnt[c] = L_t

        if c == 0:
            s.wait("tensor", s_gw, GW_ATTN)
        s.wait("tensor", s_ld, L_t)
        for ht in range(16):
            bk = p1["mm_rot"] % 2
            p1["mm_rot"] += 1
            pa_wait(bk)
            for kt in range(8):
                s.op("tensor", lambda t, ht=ht, kt=kt, bk=bk, buf=buf, q=q: t.matmul(
                    p_a[bk][:, 0:L1], w_in_sb[:, kt, ht * 128:(ht + 1) * 128],
                    xn2[buf][:, kt, q * 128:(q + 1) * 128],
                    start=(kt == 0), stop=(kt == 7)),
                    inc=(s_pe if kt == 7 else None))
            P_ht = s_pe.n
            s.wait("vector", s_pe, P_ht)
            if ht == 0:
                s.wait("vector", s_gp, p1.get("G_rot", 0))
            s.op("vector", lambda v, ht=ht, bk=bk: v.tensor_scalar(
                xh_f[:, ht, :], p_a[bk][:, 0:L1], bin_c[:, ht:ht + 1], None,
                ALU.add), inc=s_dv)
            V_xh = s_dv.n
            if ht == 0:
                s.wait("scalar", s_pe, p1["P_bdl_prev"])
                s.wait("scalar", s_gp, p1["G_carry"])
            s.wait("scalar", s_dv, V_xh)
            s.op("scalar", lambda sc, ht=ht, bk=bk: sc.activation(
                xh_b[:, ht, :], p_a[bk][:, 0:L1], AF.Identity,
                bias=bin_c[:, ht:ht + 1]), inc=s_ac)
            pa_free[bk] = ("ac", s_ac.n)

        yield
        s.wait("scalar", s_dv, p1["V_scans"])
        s.wait("scalar", s_gp, p1["G_carry"])
        for wsb, bcol, gpl in ((wgx_sb, bgx_c, GX), (wga_sb, bga_c, GA)):
            for b in range(NB):
                bk = p1["mm_rot"] % 2
                p1["mm_rot"] += 1
                pa_wait(bk)
                for kt in range(2):
                    s.op("tensor", lambda t, wsb=wsb, b=b, kt=kt, bk=bk: t.matmul(
                        p_a[bk][:, 0:L1], wsb[:, b, kt, :],
                        xh_b[:, b * 2 + kt, :], start=(kt == 0), stop=(kt == 1)),
                        inc=(s_pe if kt == 1 else None))
                P_g = s_pe.n
                s.wait("scalar", s_pe, P_g)
                s.op("scalar", lambda sc, gpl=gpl, b=b, bk=bk, bcol=bcol: sc.activation(
                    gpl[:, b, :], p_a[bk][:, 0:L1], AF.Sigmoid,
                    bias=bcol[:, b:b + 1]), inc=s_ac)
                pa_free[bk] = ("ac", s_ac.n)
        p1["P_bdl_prev"] = s_pe.n
        A_gates = s_ac.n

        yield
        if c == 0:
            s.wait("vector", s_dv, max(V_MSP, V_INIT))
        s.wait("vector", s_ac, A_gates)
        s.op("vector", lambda v: v.tensor_tensor(T0[:], GA[:], mspb, ALU.mult))
        s.op("vector", lambda v: v.tensor_tensor(T1[:], GA[:], aipb, ALU.mult),
             inc=s_dv)
        s.wait("vector", s_gp, p1["G_carry"])
        for ct in range(8):
            s.op("vector", lambda v, ct=ct: v.tensor_tensor_scan(
                T2[:, ct, :], ones_pl[:], T1[:, ct, :], 0.0, ALU.mult, ALU.add))
        s.op("vector", lambda v: v.tensor_single_scalar(
            T1[:], T2[:], 1.0 / (2 * PI), ALU.mult))
        s.op("vector", lambda v: v.tensor_copy(KIap[:], T1[:]))
        s.op("vector", lambda v: v.tensor_copy(T1[:], KIap[:]))
        s.op("vector", lambda v: v.scalar_tensor_tensor(
            T1[:], T1[:], -2 * PI, T2[:], ALU.mult, ALU.add), inc=s_dv)
        V_phi = s_dv.n
        yield
        s.wait("scalar", s_dv, V_phi)
        s.op("scalar", lambda sc: sc.activation(T2[:], T0[:], AF.Exp))
        s.op("scalar", lambda sc: sc.activation(T0[:], T2[:], AF.Square), inc=s_ac)
        A_q = s_ac.n
        s.wait("vector", s_ac, A_q)
        s.op("vector", lambda v: v.tensor_scalar(
            T0[:], T0[:], -1.0, 1.0, ALU.mult, ALU.add))
        s.op("vector", lambda v: v.tensor_single_scalar(
            T0[:], T0[:], 1e-6, ALU.max), inc=s_dv)
        V_r2 = s_dv.n
        s.wait("scalar", s_dv, V_r2)
        s.op("scalar", lambda sc: sc.activation(T3[:], T0[:], AF.Ln))
        s.op("scalar", lambda sc: sc.activation(T0[:], T3[:], AF.Exp,
                                                scale=ccol(C_HALF)), inc=s_ac)
        s.op("scalar", lambda sc: sc.activation(T3[:], T1[:], AF.Sin))
        s.op("scalar", lambda sc: sc.activation(T5[:], T1[:], AF.Abs))
        s.op("scalar", lambda sc: sc.activation(
            T4[:], T5[:], AF.Sin, scale=ccol(C_NEG1), bias=ccol(C_PIH)), inc=s_ac)
        A_trig = s_ac.n
        yield
        s.wait("gpsimd", s_ac, A_trig)
        s.wait("gpsimd", s_dv, V_phi)
        s.op("gpsimd", lambda g: g.tensor_mul(T1[:], GX[:], T0[:]))
        s.op("gpsimd", lambda g: g.tensor_mul(GX[:], T4[:], T1[:]))
        s.op("gpsimd", lambda g: g.tensor_mul(T0[:], T3[:], T1[:]))
        s.op("gpsimd", lambda g: g.tensor_mul(T1[:], GX[:], xr))
        s.op("gpsimd", lambda g: g.tensor_mul(T5[:], T0[:], xi))
        s.op("gpsimd", lambda g: g.tensor_add(T1[:], T1[:], T5[:]))
        s.op("gpsimd", lambda g: g.tensor_mul(GA[:], GX[:], xi))
        s.op("gpsimd", lambda g: g.tensor_mul(T5[:], T0[:], xr))
        s.op("gpsimd", lambda g: g.tensor_sub(GA[:], GA[:], T5[:]))
        gflush(s_gp)
        G_rot = s_gp.n
        p1["G_rot"] = G_rot
        yield
        s.wait("vector", s_gp, G_rot)
        for ct in range(8):
            s.op("vector", lambda v, ct=ct: v.tensor_tensor_scan(
                GX[:, ct, :], T2[:, ct, :], T1[:, ct, :], carry_r[:, ct:ct + 1],
                ALU.mult, ALU.add))
        for ct in range(8):
            s.op("vector", lambda v, ct=ct: v.tensor_tensor_scan(
                T0[:, ct, :], T2[:, ct, :], GA[:, ct, :], carry_i[:, ct:ct + 1],
                ALU.mult, ALU.add))
        vflush(s_dv)
        V_scans = s_dv.n
        p1["V_scans"] = V_scans
        s.wait("gpsimd", s_dv, V_scans)
        s.wait("gpsimd", s_pe, p1["P_wo_prev"])
        s.op("gpsimd", lambda g: g.tensor_mul(T2[:], T4[:], GX[:]))
        s.op("gpsimd", lambda g: g.tensor_mul(T5[:], T3[:], T0[:]))
        s.op("gpsimd", lambda g: g.tensor_sub(hr, T2[:], T5[:]))
        s.op("gpsimd", lambda g: g.tensor_mul(T2[:], T3[:], GX[:]))
        s.op("gpsimd", lambda g: g.tensor_mul(T5[:], T4[:], T0[:]))
        s.op("gpsimd", lambda g: g.tensor_add(hi, T2[:], T5[:]))
        LC = L1 - 1
        s.op("gpsimd", lambda g: g.tensor_mul(tc0[:], T4[:, :, LC], GX[:, :, LC]))
        s.op("gpsimd", lambda g: g.tensor_mul(tc1[:], T3[:, :, LC], T0[:, :, LC]))
        s.op("gpsimd", lambda g: g.tensor_mul(tc2[:], T3[:, :, LC], GX[:, :, LC]))
        s.op("gpsimd", lambda g: g.tensor_mul(tc3[:], T4[:, :, LC], T0[:, :, LC]))
        s.op("gpsimd", lambda g: g.tensor_sub(carry_r[:], tc0[:], tc1[:]))
        s.op("gpsimd", lambda g: g.tensor_add(carry_i[:], tc2[:], tc3[:]))
        gflush(s_gp)
        p1["G_carry"] = s_gp.n
        V_h = s_gp.n
        yield
        yield
        s.wait("tensor", s_gp, V_h)
        pa_wait(0)
        pa_wait(1)
        for kt in range(16):
            for nh in range(2):
                s.op("tensor", lambda t, nh=nh, kt=kt: t.matmul(
                    p_a[nh][:, 0:512], h_bf[:, kt, :],
                    w_out_sb[:, kt, nh * 512:(nh + 1) * 512],
                    start=(kt == 0), stop=(kt == 15)),
                    inc=(s_pe if (nh == 1 and kt == 15) else None))
        P_wo = s_pe.n
        p1["P_wo_prev"] = P_wo
        s.wait("vector", s_pe, P_wo)
        if p1["S_p_prev"]:
            s.wait("vector", s_st, p1["S_p_prev"])
        for nh in range(2):
            sl = slice(nh * 512, (nh + 1) * 512)
            s.op("vector", lambda v, nh=nh, sl=sl: v.scalar_tensor_tensor(
                partial[:, sl], p_a[nh][:, 0:512], 1.0, x_tm[:, sl],
                ALU.mult, ALU.add))
            s.op("vector", lambda v, sl=sl: v.tensor_add(
                partial[:, sl], partial[:, sl], bout_pl[:, sl]), inc=s_dv)
        pa_free[0] = ("dv", s_dv.n)
        pa_free[1] = ("dv", s_dv.n)
        p1["V_end"] = s_dv.n
        Vend_hist[c] = s_dv.n

        s.wait("sync", s_dv, p1["V_end"])
        s.dma("sync", pscr_h[c], partial[:], s_st)
        p1["S_p_prev"] = s_st.n
        S_pscr[c] = s_st.n

    def ph2_gen(c2):
        buf = c2 % 2
        if p2["S_part_prev"]:
            s.wait("sync", s_st, p2["S_part_prev"])
        s.wait("sync", s_st, S_pscr[4 * c2 + 3])
        for q in range(4):
            s.dma("sync", part2[:, q, :], pscr_h[4 * c2 + q], s_p2l)
        p2["P2L"] = s_p2l.n

        s.wait("tensor", s_ld, S_xnt[4 * c2 + 3])
        for po in range(2):
            if p2["po_free"][po]:
                s.wait("tensor", s_dv, p2["po_free"][po])
        for tt in range(NT2):
            for kt in range(8):
                s.op("tensor", lambda t, tt=tt, kt=kt, buf=buf: t.matmul(
                    p_o0[:, tt * E:(tt + 1) * E],
                    xn2[buf][:, kt, tt * 128:(tt + 1) * 128],
                    rw_sb[:, kt, :], start=(kt == 0), stop=(kt == 7)),
                    inc=(s_pe if (tt == NT2 - 1 and kt == 7) else None))
        P_r = s_pe.n
        s.wait("vector", s_pe, P_r)
        s.op("vector", lambda v: v.tensor_reduce(
            sm0[:], prv, AX.X, ALU.max, negate=True))
        vflush(s_dv)
        V_nm = s_dv.n
        s.wait("scalar", s_dv, V_nm)
        for tt in range(NT2):
            s.op("scalar", lambda sc, tt=tt: sc.activation(
                pr4[:, tt, :], prv[:, tt, :], AF.Exp, bias=sm0[:, tt:tt + 1]))
        aflush(s_ac)
        A_exp = s_ac.n
        s.wait("vector", s_ac, A_exp)
        s.op("vector", lambda v: v.reduce_sum(sm1[:], pr4[:], AX.X), inc=s_dv)
        p2["po_free"][0] = s_dv.n
        vsp()
        s.op("vector", lambda v: v.reciprocal(sm0[:], sm1[:]))
        vsp()
        s.op("vector", lambda v: v.tensor_tensor(
            pr4[:], pr4[:], sm0.ap().broadcast_to([128, NT2, E]), ALU.mult))
        vsp()
        s.op("vector", lambda v: v.tensor_reduce(sm1[:], pr4[:], AX.X, ALU.max))
        vsp()
        s.op("vector", lambda v: v.tensor_tensor(
            lp[:], pr4[:], sm1.ap().broadcast_to([128, NT2, E]), ALU.is_lt))
        vsp()
        s.op("vector", lambda v: v.tensor_mul(lp[:], lp[:], pr4[:]))
        vsp()
        s.op("vector", lambda v: v.tensor_reduce(sm1[:], lp[:], AX.X, ALU.max))
        vsp()
        s.op("vector", lambda v: v.tensor_tensor(
            lp[:], pr4[:], sm1.ap().broadcast_to([128, NT2, E]), ALU.is_ge))
        vsp()
        s.op("vector", lambda v: v.tensor_mul(lp[:], lp[:], pr4[:]))
        vflush(s_dv)
        p2["V_lp"] = s_dv.n
        yield
        if debug and c2 == dbgc2:
            s.wait("sync", s_dv, p2["V_lp"])
            s.dma("sync", dbg_lp, lp[:], s_ld)
            s.dma("sync", dbg_pr4, pr4[:], s_ld)
            s.dma("sync", dbg_sm, sm0[:], s_ld)
            s.dma("sync", dbg_xn2, xn2[c2 % 2][:], s_ld)
            s.dma("sync", dbg_part2, part2[:], s_ld)
            s.wait("vector", s_ld)
            s.wait("scalar", s_ld)
        for e in range(E):
            for _u in ph2_expert(c2, e):
                yield
        V_acc = s_dv.n
        s.wait("sync", s_dv, V_acc)
        s.dma("sync", y_r[c2], part2[:], s_st)
        p2["S_part_prev"] = s_st.n

    def ph2_expert(c2, e):
        buf = c2 % 2
        if c2 == 0:
            s.wait("sync", s_gw, GW_CONV_E[e])
        G_hh = []
        for hh in range(2):
            wslot = p2["ghh"] % W2R_
            p2["ghh"] += 1
            if p2["w2_free"][wslot]:
                s.wait("sync", s_pe, p2["w2_free"][wslot])
            s.dma("sync", w2r[:, wslot], w2bf_h[e, hh], s_w2s[wslot])
            G_hh.append((wslot, s_w2s[wslot].n))

        if e == 0:
            s.wait("tensor", s_dv, p2["V_lp"])
        for ft in range(11):
            slot = p2["gft"] % W1R_
            p2["gft"] += 1
            if p2["w1_free"][slot]:
                s.wait("sync", s_pe, p2["w1_free"][slot])
            s.dma("sync", w1r[:, slot], w1bf_h[e, ft], s_w1s[slot])
            gval = s_w1s[slot].n
            par = ft % 2
            s.wait("tensor", s_w1s[slot], gval)
            if p2["py_free"][par]:
                s.wait("tensor", s_dv, p2["py_free"][par])
            for half in range(2):
                for kt in range(8):
                    s.op("tensor", lambda t, slot=slot, half=half, kt=kt, par=par,
                         buf=buf: t.matmul(
                        p_y[par][half][:],
                        w1r[:, slot, half, kt, :],
                        xn2[buf][:, kt, :],
                        start=(kt == 0), stop=(kt == 7)),
                        inc=(s_pe if (half == 1 and kt == 7) else None))
            P_yft = s_pe.n
            p2["w1_free"][slot] = P_yft
            s.wait("scalar", s_pe, P_yft)
            if p2["py_free"][par ^ 1]:
                s.wait("scalar", s_dv, p2["py_free"][par ^ 1])
            s.op("scalar", lambda sc, par=par: sc.activation(
                silu_sb[par][:], p_y[par][0][:], AF.Silu), inc=s_ac)
            A_sil = s_ac.n
            s.wait("vector", s_ac, A_sil)
            s.op("vector", lambda v, ft=ft, par=par: v.tensor_mul(
                gbuf[:, ft, :], silu_sb[par][:], p_y[par][1][:]), inc=s_dv)
            p2["py_free"][par] = s_dv.n
            yield
        V_g = s_dv.n
        if debug and c2 == dbgc2 and e == 0:
            s.wait("sync", s_dv, V_g)
            s.dma("sync", dbg_g, gbuf[:], s_ld)
            s.wait("tensor", s_ld)
        if e == E - 1:
            p2["xn2_freeby"][buf] = s_pe.n

        s.wait("tensor", s_dv, V_g)
        if e == 0:
            s.wait("vector", s_p2l, p2["P2L"])
        for hh in range(2):
            wslot, gval = G_hh[hh]
            s.wait("tensor", s_w2s[wslot], gval)
            for tt in range(NT2):
                po = p2["gtt"] % 2
                p2["gtt"] += 1
                if p2["po_free"][po]:
                    s.wait("tensor", s_dv, p2["po_free"][po])
                for ft in range(11):
                    s.op("tensor", lambda t, tt=tt, ft=ft, po=po, wslot=wslot: t.matmul(
                        p_o[po][:], gbuf[:, ft, tt * 128:(tt + 1) * 128],
                        w2r[:, wslot, ft, :],
                        start=(ft == 0), stop=(ft == 10)),
                        inc=(s_pe if ft == 10 else None))
                P_oms = s_pe.n
                s.wait("vector", s_pe, P_oms)
                sl = slice(hh * 512, (hh + 1) * 512)
                s.op("vector", lambda v, tt=tt, po=po, e=e, sl=sl: v.scalar_tensor_tensor(
                    part2[:, tt, sl], p_o[po][:], lp[:, tt, e:e + 1],
                    part2[:, tt, sl], ALU.mult, ALU.add), inc=s_dv)
                p2["po_free"][po] = s_dv.n
                yield
            p2["w2_free"][wslot] = s_pe.n
        if debug and c2 == dbgc2 and e == 0:
            s.wait("sync", s_dv, s_dv.n)
            s.dma("sync", dbg_p2, part2[:], s_ld)
            s.wait("tensor", s_ld)
            s.wait("vector", s_ld)

    ph2q = []
    next_c2 = 0

    def pump_ph2(n):
        emitted = 0
        while emitted < n and ph2q:
            try:
                next(ph2q[0][1])
                emitted += 1
            except StopIteration:
                ph2q.pop(0)

    for k in range(NC1):
        while ph2q and ph2q[0][0] <= k // 4 - 2:
            try:
                next(ph2q[0][1])
            except StopIteration:
                ph2q.pop(0)
        for _u in ph1_gen(k):
            pump_ph2(5)
        if next_c2 < NC2 and k >= 4 * next_c2 + PH2_LAG - 1:
            ph2q.append((next_c2, ph2_gen(next_c2)))
            next_c2 += 1
        pump_ph2(5)
    while next_c2 < NC2:
        ph2q.append((next_c2, ph2_gen(next_c2)))
        next_c2 += 1
    while ph2q:
        try:
            next(ph2q[0][1])
        except StopIteration:
            ph2q.pop(0)

    s.wait("sync", s_st, s_st.n)

    s.emit(nc, "mk")
    return nc


def make_in_maps(L0_attn_x, L0_attn_w_in, L0_attn_b_in, L0_attn_w_gx, L0_attn_b_gx,
                 L0_attn_w_ga, L0_attn_b_ga, L0_attn_a_real_param, L0_attn_a_imag_param,
                 L0_attn_w_out, L0_attn_b_out, L0_ffn_router_weight, L0_ffn_w1,
                 L0_ffn_w2):
    f32 = np.float32
    cc = np.zeros((128, 8), f32)
    cc[:, C_ONE] = 1.0
    cc[:, C_NEG1] = -1.0
    cc[:, C_HALF] = 0.5
    cc[:, C_NEGHALF] = -0.5
    cc[:, C_PIH] = PI / 2
    cc[:, C_EPS] = 1e-5
    cc[:, C_NEG2PI] = -2 * PI
    cc[:, C_I2PI] = 1.0 / (2 * PI)
    shared = dict(
        w_in=np.ascontiguousarray(L0_attn_w_in, f32),
        b_in=np.ascontiguousarray(np.asarray(L0_attn_b_in, f32).reshape(16, 128).T),
        w_gx=np.ascontiguousarray(L0_attn_w_gx, f32),
        b_gx=np.ascontiguousarray(np.asarray(L0_attn_b_gx, f32).reshape(8, 128).T),
        w_ga=np.ascontiguousarray(L0_attn_w_ga, f32),
        b_ga=np.ascontiguousarray(np.asarray(L0_attn_b_ga, f32).reshape(8, 128).T),
        arp=np.ascontiguousarray(np.asarray(L0_attn_a_real_param, f32).reshape(8, 128).T),
        aip=np.ascontiguousarray(np.asarray(L0_attn_a_imag_param, f32).reshape(8, 128).T),
        w_out=np.ascontiguousarray(L0_attn_w_out, f32),
        bout_pl=np.ascontiguousarray(
            np.broadcast_to(np.asarray(L0_attn_b_out, f32), (128, D))),
        rw=np.ascontiguousarray(L0_ffn_router_weight, f32),
        w1=np.ascontiguousarray(L0_ffn_w1, f32),
        w2=np.ascontiguousarray(L0_ffn_w2, f32),
        cc=cc,
    )
    x = np.ascontiguousarray(np.asarray(L0_attn_x, f32))
    in_maps = []
    for i in range(8):
        m = dict(shared)
        m["x"] = np.ascontiguousarray(x[2 * i:2 * i + 2])
        in_maps.append(m)
    return in_maps


def kernel(**inputs):
    in_maps = make_in_maps(**inputs)
    nc = build_nc()
    res = run_bass_kernel_spmd(nc, in_maps, core_ids=list(range(8)))
    return np.concatenate([np.asarray(r["y"], np.float32) for r in res.results], axis=0)
